# version 15
# baseline (speedup 1.0000x reference)
import contextlib
import numpy as np

import concourse.bass as bass
import concourse.mybir as mybir
from concourse.bass_utils import run_bass_kernel_spmd

F32 = mybir.dt.float32
BF16 = mybir.dt.bfloat16
I32 = mybir.dt.int32
AF = mybir.ActivationFunctionType
ALU = mybir.AluOpType
AX = mybir.AxisListType

NCORES = 8
HID = 2048
T = 1024
NKC = 16
NKV = 8
NQH = 16
D = 128
NCH = 8
NTS = 8
S16 = 1.0 / 16.0
LNSQD = float(np.log(np.sqrt(128.0)))

KGROUPS = [(0, 3), (3, 3), (6, 2)]
QGROUPS = [(0, 3), (3, 3), (6, 3), (9, 3), (12, 3), (15, 1)]
QGRP_OF_KVH = [0, 1, 1, 2, 3, 3, 4, 5]


class _Sem:
    __slots__ = ("h", "n")

    def __init__(self, es, nc, name):
        self.h = es.enter_context(nc.semaphore(name))
        self.n = 0

    def inc(self, inst, k=1):
        self.n += k
        inst.then_inc(self.h, k)
        return self.n


class Ring:

    def __init__(self, es, nc, name, n, cons_per=1, data_only=False):
        self.rdy = _Sem(es, nc, name + "_r")
        self.fre = None if data_only else _Sem(es, nc, name + "_f")
        self.n = n
        self.cons_per = cons_per
        self.ip = 0
        self.tgts = []
        self._ic0 = 0

    def prod_wait(self, eng):
        if self.fre is not None and self.ip >= self.n:
            eng.wait_ge(self.fre.h, self.cons_per * (self.ip - self.n + 1))
        return self.ip % self.n

    def prod_done(self, inst):
        self.rdy.inc(inst)
        if self.ip < len(self.tgts):
            assert self.tgts[self.ip] == self.rdy.n
        else:
            self.tgts.append(self.rdy.n)
        self.ip += 1

    def prefill(self, count):
        self.tgts = self.tgts + [self.rdy.n + i + 1 for i in range(count)]

    def cons_wait(self, eng):
        slot = self._ic0 % self.n
        eng.wait_ge(self.rdy.h, self.tgts[self._ic0])
        self._ic0 += 1
        return slot

    def cons_done(self, inst):
        self.fre.inc(inst)

    def consumer(self):
        ring = self

        class _C:
            def __init__(self):
                self.ic = 0

            def wait(self, eng):
                slot = self.ic % ring.n
                eng.wait_ge(ring.rdy.h, ring.tgts[self.ic])
                self.ic += 1
                return slot

        return _C()


class TileStream:
    def __init__(self, es, nc, name, nbuf=2):
        self.ld = [_Sem(es, nc, f"{name}_l{i}") for i in range(nbuf)]
        self.fre = _Sem(es, nc, name + "_f")
        self.nbuf = nbuf
        self.ti = 0
        self.tc = 0
        self.tgts = []

    def issue(self, eng, emit):
        t = self.ti
        p = t % self.nbuf
        if t >= self.nbuf:
            eng.wait_ge(self.fre.h, t - self.nbuf + 1)
        for i in emit(p):
            self.ld[p].inc(i, 16)
        self.tgts.append((p, self.ld[p].n))
        self.ti += 1

    def wait(self, eng):
        p, tgt = self.tgts[self.tc]
        eng.wait_ge(self.ld[p].h, tgt)
        self.tc += 1
        return p

    def done(self, inst):
        self.fre.inc(inst)

    def done_skip(self):
        pass


class PQ:

    def __init__(self):
        self.q = []

    def defer(self, fn):
        self.q.append(fn)

    def put(self, inst, fn=None):
        if fn is not None:
            fn(inst)
        elif self.q:
            self.q.pop(0)(inst)
        return inst

    def empty(self):
        return not self.q


def _build(dbg_x2=False):
    nc = bass.Bass("TRN2", target_bir_lowering=False, debug=False,
                   detect_race_conditions=False)

    def din(name, shape, dtype=F32):
        return nc.dram_tensor(name, shape, dtype, kind="ExternalInput")

    xfm = din("xfm", [HID, T])
    xtm = din("xtm", [T, HID])
    wq_d = din("wq", [HID, HID])
    wk_d = din("wk", [HID, NKV * D])
    wv_d = din("wv", [HID, NKV * D])
    wg1_d = din("wg1", [HID, 16])
    wg2_d = din("wg2", [16, NKV * D])
    wo_d = din("wo", [HID, HID])
    wt_d = din("wt", [HID, HID])
    bqr = din("bqr", [128, 16])
    bkr = din("bkr", [128, 8])
    bg2r = din("bg2r", [128, 8])
    bvr = din("bvr", [1, NKV * D])
    gnw_d = din("gnw", [128, 1])
    um_d = din("um", [128, 8])
    out_d = nc.dram_tensor("out", [T, HID], F32, kind="ExternalOutput")
    cc_in = nc.dram_tensor("cc_in", [128, 1032], F32)
    cc_out = nc.dram_tensor("cc_out", [NCORES * 128, 1032], F32)
    tapx2 = (nc.dram_tensor("tapx2", [T, HID], F32, kind="ExternalOutput")
             if dbg_x2 else None)
    tapo = (nc.dram_tensor("tapo", [128, 16384], mybir.dt.bfloat16, kind="ExternalOutput")
            if dbg_x2 else None)
    tapT7 = (nc.dram_tensor("tapT7", [128, 8192], mybir.dt.bfloat16, kind="ExternalOutput")
             if dbg_x2 else None)
    tapTab = (nc.dram_tensor("tapTab", [128, 328], F32, kind="ExternalOutput")
              if dbg_x2 else None)
    tapB = (nc.dram_tensor("tapB", [128, 1024], F32, kind="ExternalOutput")
            if dbg_x2 else None)
    tapxr = (nc.dram_tensor("tapxr", [T, HID], F32, kind="ExternalOutput")
             if dbg_x2 else None)
    tapAB = (nc.dram_tensor("tapAB", [128, 22], F32, kind="ExternalOutput")
             if dbg_x2 else None)

    es = contextlib.ExitStack()
    sbt = lambda name, shape, dtype: es.enter_context(nc.sbuf_tensor(name, shape, dtype))

    iota_f = sbt("iota_f", [128, 384], BF16)
    maskU = sbt("maskU", [128, 128], BF16)
    sel = sbt("sel", [128, 256], BF16)
    ones_b = sbt("ones_b", [1, 128], BF16)
    ones_b3 = sbt("ones_b3", [65, 128], BF16)
    ones_cf = sbt("ones_cf", [128, 1], F32)
    cb = sbt("cb", [128, 2], F32)
    bq_s = sbt("bq_s", [128, 16], F32)
    bk_s = sbt("bk_s", [128, 8], F32)
    bg2_s = sbt("bg2_s", [128, 8], F32)
    bv_b = sbt("bv_b", [1, NKV * D], BF16)
    gnw_s = sbt("gnw_s", [128, 1], F32)
    um_s = sbt("um_s", [128, 8], F32)
    mtab = sbt("mtab", [128, 64], F32)
    ftab = sbt("ftab", [128, 64], F32)
    bkt = sbt("bkt", [128, 64], F32)
    bqt = sbt("bqt", [128, 64], F32)
    emid = sbt("emid", [128, 64], F32)
    sendf = sbt("sendf", [128, 8], F32)
    acc8 = sbt("acc8", [128, 8], F32)
    w8 = sbt("w8", [128, 8], F32)
    bsave = sbt("bsave", [128, 8], F32)
    alpha_c = sbt("alpha_c", [128, 2], F32)
    a11 = sbt("a11", [1, 4], F32)
    wsum = sbt("wsum", [128, 17], F32)
    abcols = sbt("abcols", [128, 16], F32)
    bn2 = sbt("bn2", [128, 16], F32)
    lncol = sbt("lncol", [128, 32], F32)

    AR1 = sbt("AR1", [128, 16384], BF16)
    AR2 = sbt("AR2", [128, 16384], BF16)
    AR3 = sbt("AR3", [128, 8192], F32)
    AR4 = sbt("AR4", [128, 8192], F32)
    AR5 = sbt("AR5", [128, 8192], BF16)
    AR6 = sbt("AR6", [128, 8192], BF16)
    AR7 = sbt("AR7", [128, 8192], BF16)
    AR8 = sbt("AR8", [128, 8192], BF16)
    AR9 = sbt("AR9", [128, 4864], BF16)

    x_bf = lambda kc: AR1[:, kc * 1024:(kc + 1) * 1024]
    o_bf = lambda qh: AR1[:, qh * 1024:(qh + 1) * 1024]
    ux_f32 = lambda p: AR1.bitcast(F32)[:, p * 1024:(p + 1) * 1024]
    ix_i32 = lambda p: AR1.bitcast(I32)[:, 2048 + p * 1024:2048 + (p + 1) * 1024]
    x2q_tm = lambda p: AR1[:, 8192 + p * 2048:8192 + (p + 1) * 2048]
    out_stg = lambda p: AR1.bitcast(F32)[:, 2048 + p * 2048:2048 + (p + 1) * 2048]

    q_bf = lambda qh: AR2[:, qh * 1024:(qh + 1) * 1024]
    wv_r = lambda kc: AR2[:, kc * 1024:(kc + 1) * 1024]

    B_t = lambda kvh: AR3[:, kvh * 1024:(kvh + 1) * 1024]
    k_rot = lambda p: AR4[:, p * 1024:(p + 1) * 1024]
    T_loc = lambda kvh: AR4[:, 3072 + kvh * 128:3072 + (kvh + 1) * 128]
    S_init = lambda kvh: AR4[:, 4096 + kvh * 128:4096 + (kvh + 1) * 128]
    gth = lambda p: AR4[:, 5120 + p * 1032:5120 + (p + 1) * 1032]
    ccpack = gth(0)
    xr = lambda ts: (AR3 if ts < 4 else AR4)[:, (ts % 4) * 2048:(ts % 4 + 1) * 2048]

    kt_t = lambda kvh: AR5[:, kvh * 1024:(kvh + 1) * 1024]
    v_tm = lambda ts: AR6[:, ts * 1024:(ts + 1) * 1024]
    T_bf = lambda kvh, c: AR7[:, (kvh * 8 + c) * 128:(kvh * 8 + c + 1) * 128]

    def wtq(kc):
        if kc < 8:
            return AR2[:, kc * 2048:(kc + 1) * 2048]
        if kc < 12:
            return AR5[:, (kc - 8) * 2048:(kc - 7) * 2048]
        return AR6[:, (kc - 12) * 2048:(kc - 11) * 2048]

    x2q_fm = lambda kc: (AR7 if kc < 8 else AR8)[:, (kc % 8) * 1024:(kc % 8 + 1) * 1024]
    wtf = lambda p: AR8.bitcast(F32)[:, p * 2048:(p + 1) * 2048]
    stat_sb = AR8.bitcast(F32)[0:16, 0:1024]
    r_sb = AR8.bitcast(F32)[0:16, 1024:2048]
    rscr = AR8.bitcast(F32)[0:16, 2048:3072]
    rhl = AR8[0:16, 6144:8192]
    rstage = lambda p: AR8[:, 2048 + p * 2048:4096 + p * 2048]
    uw_f32 = lambda p: AR7.bitcast(F32)[:, p * 512:(p + 1) * 512]
    iw_i32 = lambda p: AR7.bitcast(I32)[:, 1024 + p * 512:1024 + (p + 1) * 512]

    wg1_b = AR9[:, 0:256]
    bn6 = AR9.bitcast(F32)[:, 128:320]
    wstr = lambda p: AR9[:, 256 + p * 512:256 + (p + 1) * 512]
    ktr = lambda p: AR9[:, 1280 + p * 128:1280 + (p + 1) * 128]
    A_b = lambda p: AR9[:, 1536 + p * 128:1536 + (p + 1) * 128]
    osq_b = lambda p: AR9[:, 1792 + p * 128:1792 + (p + 1) * 128]
    qt_b = lambda p: AR9[:, 2048 + p * 128:2048 + (p + 1) * 128]
    ttot_b = ktr
    eb_f = lambda p: AR9.bitcast(F32)[:, 1152 + p * 128:1152 + (p + 1) * 128]
    wg2_b = AR9[0:16, 2816:3840]
    t_bf16 = AR9[0:16, 1280:2304]
    wostr = lambda p: (AR9[:, 2816:4864] if p == 0 else AR7[:, 4096:6144])

    ones_rf = iota_f.bitcast(F32)[0:1, 0:128]
    ident = iota_f[:, 256:384]
    psA = es.enter_context(nc.psum_tensor("psA", [128, 4096], F32))
    pview = lambda b, n=512: psA[:, b * 512:b * 512 + n]
    ps_bf = psA.bitcast(BF16)
    tr_bf = lambda sl: ps_bf[:, (6 + sl) * 1024:(6 + sl) * 1024 + 128]

    sems = {}
    for s in ["dGm", "dAcc", "dT", "dO", "dR", "tns", "ve", "se", "ge", "ccs"]:
        sems[s] = _Sem(es, nc, s)
    dTap = _Sem(es, nc, "dTap") if dbg_x2 else None
    dGm, dAcc, dT, dO, dR = (sems[k] for k in ["dGm", "dAcc", "dT", "dO", "dR"])
    tns, ve, se, ge, ccs = (sems[k] for k in ["tns", "ve", "se", "ge", "ccs"])

    s_wv = TileStream(es, nc, "swv", nbuf=4)
    s_wk = TileStream(es, nc, "swk")
    s_wq = TileStream(es, nc, "swq")
    s_wt1 = TileStream(es, nc, "swt1")
    s_wt2 = TileStream(es, nc, "swt2")
    s_wo = TileStream(es, nc, "swo")
    s_gth = TileStream(es, nc, "sgth")
    s_rst = TileStream(es, nc, "srst")
    r_vp = Ring(es, nc, "rvp", 2)
    r_kj = Ring(es, nc, "rkj", 1)
    r_ebem = Ring(es, nc, "rebem", 2)
    r_tr = Ring(es, nc, "rtr", 2, data_only=True)
    r_kt = Ring(es, nc, "rkt", 2, data_only=True)
    r_p = Ring(es, nc, "rp", 2, data_only=True)
    r_qj = Ring(es, nc, "rqj", 1)
    r_qt = Ring(es, nc, "rqt", 2, data_only=True)
    r_A = Ring(es, nc, "rA", 2, data_only=True)
    r_Ab = Ring(es, nc, "rAb", 2, data_only=True)
    r_T = Ring(es, nc, "rT", 2, data_only=True)
    r_O = Ring(es, nc, "rO", 2, data_only=True)
    rO_ve = r_O.consumer()
    r_sq = Ring(es, nc, "rsq", 2, data_only=True)
    r_stp = Ring(es, nc, "rstp", 2, data_only=True)
    r_rb = Ring(es, nc, "rrb", 2)
    r_uq = Ring(es, nc, "ruq", 2)
    r_wop = Ring(es, nc, "rwop", 1)
    r_ffp = Ring(es, nc, "rffp", 1)
    r_ux = Ring(es, nc, "rux", 2)
    r_xq = Ring(es, nc, "rxq", 2, data_only=True)
    r_ot = Ring(es, nc, "rot", 2, data_only=True)

    blk = es.enter_context(nc.Block())

    @blk.gpsimd
    def _(g):
        g.iota(iota_f[:, 0:128], [[1, 128]], channel_multiplier=-1,
               allow_small_or_imprecise_dtypes=True)
        g.iota(iota_f[:, 128:384], [[-1, 16], [1, 16]], channel_multiplier=0,
               allow_small_or_imprecise_dtypes=True)
        g.memset(ones_b[:, :], 1.0)
        g.memset(ones_b3[:, :], 1.0)
        g.memset(ones_cf[:, :], 1.0)
        g.memset(cb[:, 0:1], 0.0)
        g.memset(cb[:, 1:2], -LNSQD)
        g.memset(AR7[:, :], 0.0)
        g.memset(AR4[:, 4096:5120], 0.0)
        g.memset(acc8[:, :], 0.0)
        ge.inc(g.memset(stat_sb, 0.0))
        dGm.inc(g.dma_start(out=bq_s[:, :], in_=bqr[:, :]), 16)
        dGm.inc(g.dma_start(out=bk_s[:, :], in_=bkr[:, :]), 16)
        dGm.inc(g.dma_start(out=bg2_s[:, :], in_=bg2r[:, :]), 16)
        dGm.inc(g.dma_start(out=bv_b[:, :], in_=bvr[:, :]), 16)
        dGm.inc(g.dma_start(out=gnw_s[:, :], in_=gnw_d[:, :]), 16)
        dGm.inc(g.dma_start(out=um_s[:, :], in_=um_d[:, :]), 16)
        for j in range(NKC):
            dGm.inc(g.dma_start(out=wg1_b[:, j * 16:(j + 1) * 16],
                                in_=wg1_d[j * 128:(j + 1) * 128, :]), 16)
        dGm.inc(g.dma_start(out=wg2_b[:, :], in_=wg2_d[:, :]), 16)
        dGm_xb = dGm.n
        for kc in range(NKC):
            if kc >= 4:
                g.wait_ge(dGm.h, dGm_xb + 16 * (kc - 3))
            dGm.inc(g.dma_start(out=x_bf(kc), in_=xfm[kc * 128:(kc + 1) * 128, :]), 16)

    dGm_init = dGm.n

    @blk.vector
    def _(v):
        v.wait_ge(ge.h, ge.n)
        v.tensor_scalar(maskU[:, :], iota_f[:, 0:128], 0.0, None, ALU.is_ge)
        v.tensor_scalar(sel[:, :], iota_f[:, 128:384], 0.0, None, ALU.is_equal)
        ve.inc(v.tensor_scalar(ident[:, :], iota_f[:, 0:128], 0.0, None, ALU.is_equal))
    ve_masks = ve.n

    @blk.tensor
    def _(t):
        t.wait_ge(dGm.h, dGm_init)
        t.wait_ge(ve.h, ve_masks)
        for h in range(2):
            for kc in range(NKC):
                inst = t.matmul(psA[0:16, h * 512:(h + 1) * 512],
                                wg1_b[:, kc * 16:(kc + 1) * 16],
                                x_bf(kc)[:, h * 512:(h + 1) * 512],
                                start=(kc == 0), stop=(kc == NKC - 1))
        tns.inc(inst)
    tns_t = tns.n


    @blk.scalar
    def _(s):
        s.wait_ge(tns.h, tns_t)
        se.inc(s.copy(t_bf16[:, :], psA[0:16, 0:1024]))
    se_t = se.n


    @blk.tensor
    def _(t):
        t.wait_ge(se.h, se_t)
        for kvh in range(NKV):
            if kvh >= 1:
                t.wait_ge(se.h, se_t + 2 * kvh)
            for h in range(2):
                inst = t.matmul(pview(2 + h), wg2_b[:, kvh * 128:(kvh + 1) * 128],
                                t_bf16[:, h * 512:(h + 1) * 512],
                                start=True, stop=True)
            tns.inc(inst)
    tns_g0 = tns.n - NKV


    @blk.scalar
    def _(s):
        for kvh in range(NKV):
            s.wait_ge(tns.h, tns_g0 + 1 + kvh)
            if kvh >= 2:
                s.wait_ge(ve.h, ve_masks + 4 * (kvh - 1))
            ls = k_rot(kvh % 2)
            se.inc(s.activation(ls, psA[:, 1024:2048], AF.Sigmoid,
                                bias=bg2_s[:, kvh:kvh + 1], scale=1.0))
            se.inc(s.activation(ls, ls, AF.Ln, bias=cb[:, 0:1], scale=1.0))

    @blk.vector
    def _(v):
        for kvh in range(NKV):
            v.wait_ge(se.h, se_t + 2 * (kvh + 1))
            ve.inc(v.tensor_tensor_scan(B_t(kvh), k_rot(kvh % 2), k_rot(kvh % 2),
                                        0.0, ALU.add, ALU.bypass))
            m8 = mtab[:, kvh * 8:(kvh + 1) * 8]
            ve.inc(v.tensor_copy(m8, B_t(kvh)[:, 63:1024:128]))
            ve.inc(v.tensor_sub(ftab[:, kvh * 8 + 1:kvh * 8 + 8],
                                B_t(kvh)[:, 191:1024:128],
                                B_t(kvh)[:, 63:896:128]))
            ve.inc(v.tensor_sub(sendf[:, kvh:kvh + 1],
                                B_t(kvh)[:, 1023:1024], B_t(kvh)[:, 959:960]))
    ve_B0 = ve.n - 4 * NKV

    @blk.scalar
    def _(s):
        for kvh in range(NKV):
            s.wait_ge(ve.h, ve_B0 + 4 * (kvh + 1))
            c0, c1 = kvh * 8, (kvh + 1) * 8
            s.activation(ftab[:, c0 + 1:c1], ftab[:, c0 + 1:c1], AF.Exp,
                         bias=cb[:, 0:1], scale=S16)
            s.activation(sendf[:, kvh:kvh + 1], sendf[:, kvh:kvh + 1], AF.Exp,
                         bias=cb[:, 0:1], scale=S16)
            s.activation(bkt[:, c0:c1], mtab[:, c0:c1], AF.Identity,
                         bias=cb[:, 0:1], scale=S16)
            s.activation(bqt[:, c0:c1], mtab[:, c0:c1], AF.Identity,
                         bias=cb[:, 1:2], scale=-S16)
            se.inc(s.activation(emid[:, c0:c1], mtab[:, c0:c1], AF.Exp,
                                bias=cb[:, 0:1], scale=S16))
    se_tab0 = se.n - NKV

    @blk.gpsimd
    def _(g):
        for kc in range(0, NKC, 4):
            if kc >= 4:
                p_prev, tgt_prev = s_wv.tgts[kc // 4 - 1]
                g.wait_ge(s_wv.ld[p_prev].h, tgt_prev)
            s_wv.issue(g, lambda p, kc=kc: [
                g.dma_start(out=wv_r(kc + i),
                            in_=wv_d[(kc + i) * 128:(kc + i + 1) * 128, :])
                for i in range(4)])

    @blk.tensor
    def _(t):
        for _ in range(4):
            s_wv.wait(t)
        for ts in range(NTS):
            slot = r_vp.prod_wait(t)
            b0 = 4 + slot * 2
            for kc in range(NKC):
                for h in range(2):
                    t.matmul(pview(b0 + h), x_bf(kc)[:, ts * 128:(ts + 1) * 128],
                             wv_r(kc)[:, h * 512:(h + 1) * 512],
                             start=(kc == 0), stop=False)
            for h in range(2):
                inst = t.matmul(pview(b0 + h), ones_b[0:1, :],
                                bv_b[0:1, h * 512:(h + 1) * 512],
                                start=False, stop=True)
            r_vp.prod_done(inst)
        for _ in range(4):
            s_wv.done_skip()

    @blk.vector
    def _(v):
        for ts in range(NTS):
            slot = r_vp.cons_wait(v)
            b0 = 4 + slot * 2
            inst = v.tensor_copy(v_tm(ts), psA[:, b0 * 512:(b0 + 2) * 512])
            r_vp.cons_done(inst)

    @blk.gpsimd
    def _(g):
        for gi, (h0, gn) in enumerate(KGROUPS):
            for kc in range(NKC):
                s_wk.issue(g, lambda p, kc=kc, h0=h0, gn=gn: [
                    g.dma_start(out=wstr(p)[:, 0:gn * 128],
                                in_=wk_d[kc * 128:(kc + 1) * 128,
                                         h0 * 128:(h0 + gn) * 128])])

    pq_t = PQ()
    pq_v = PQ()

    @blk.tensor
    def _(t):
        t.wait_ge(r_vp.fre.h, NTS)
        t.wait_ge(se.h, se_t + 2 * NKV)
        for gi, (h0, gn) in enumerate(KGROUPS):
            r_kj.prod_wait(t)
            for kc in range(NKC):
                p = s_wk.wait(t)
                for hh in range(gn):
                    for half in range(2):
                        inst = t.matmul(pview(hh * 2 + half),
                                        wstr(p)[:, hh * 128:(hh + 1) * 128],
                                        x_bf(kc)[:, half * 512:(half + 1) * 512],
                                        start=(kc == 0), stop=(kc == NKC - 1))
                        if (kc, hh, half) == (0, 0, 0):
                            pq_t.put(inst)
                if kc < NKC - 1:
                    s_wk.done(inst)
            r_kj.prod_done(inst)
            if gi < len(KGROUPS) - 1:
                pq_t.defer(s_wk.done)
            else:
                s_wk.done_skip()

    @blk.scalar
    def _(s):
        for gi, (h0, gn) in enumerate(KGROUPS):
            r_kj.cons_wait(s)
            for hh in range(gn):
                kvh = h0 + hh
                ks = kvh % 3
                if kvh >= 3:
                    s.wait_ge(r_ebem.fre.h, 8 * (kvh - 2))
                for half in range(2):
                    inst = s.activation(k_rot(ks)[:, half * 512:(half + 1) * 512],
                                        pview(hh * 2 + half), AF.Relu,
                                        bias=bk_s[:, kvh:kvh + 1], scale=1.0)
            r_kj.cons_done(inst)
            for hh in range(gn):
                kvh = h0 + hh
                s.wait_ge(se.h, se_tab0 + kvh + 1)
                for c in range(NCH):
                    p = r_ebem.prod_wait(s)
                    r_ebem.prod_done(
                        s.activation(eb_f(p), B_t(kvh)[:, c * 128:(c + 1) * 128],
                                     AF.Exp,
                                     bias=bkt[:, kvh * 8 + c:kvh * 8 + c + 1],
                                     scale=-S16))

    @blk.vector
    def _(v):
        for gi, (h0, gn) in enumerate(KGROUPS):
            for hh in range(gn):
                kvh = h0 + hh
                ks = kvh % 3
                for c in range(NCH):
                    p = r_ebem.cons_wait(v)
                    inst = v.tensor_mul(kt_t(kvh)[:, c * 128:(c + 1) * 128],
                                        k_rot(ks)[:, c * 128:(c + 1) * 128], eb_f(p))
                    r_ebem.cons_done(inst)
    ktilde_fre = lambda kvh: 8 * (kvh + 1)

    r_kt.prefill(NKV * NCH)

    @blk.tensor
    def _(t):
        for kvh in range(NKV):
            t.wait_ge(r_ebem.fre.h, ktilde_fre(kvh))
            for c in range(NCH):
                sl = r_tr.prod_wait(t)
                r_tr.prod_done(t.transpose(tr_bf(sl),
                                           kt_t(kvh)[:, c * 128:(c + 1) * 128],
                                           ident[:, :]))
                ksl = r_kt.cons_wait(t)
                psl = r_p.prod_wait(t)
                r_p.prod_done(t.matmul(pview(4 + psl, 128), ktr(ksl),
                                       v_tm(c)[:, kvh * 128:(kvh + 1) * 128],
                                       start=True, stop=True))

    @blk.vector
    def _(v):
        for kvh in range(NKV):
            for c in range(NCH):
                sl = r_tr.cons_wait(v)
                ksl = r_kt.prod_wait(v)
                r_kt.prod_done(v.tensor_copy(ktr(ksl), tr_bf(sl)))
                psl = r_p.cons_wait(v)
                pp = pview(4 + psl, 128)
                if c == 0:
                    v.tensor_copy(T_loc(kvh), pp)
                    v.tensor_scalar(T_loc(kvh), T_loc(kvh),
                                    ftab[:, kvh * 8 + 1:kvh * 8 + 2],
                                    None, ALU.mult)
                    v.tensor_copy(T_bf(kvh, 1), T_loc(kvh))
                elif c < NCH - 1:
                    v.tensor_add(T_loc(kvh), pp, T_loc(kvh))
                    v.tensor_scalar(T_loc(kvh), T_loc(kvh),
                                    ftab[:, kvh * 8 + c + 1:kvh * 8 + c + 2],
                                    None, ALU.mult)
                    v.tensor_copy(T_bf(kvh, c + 1), T_loc(kvh))
                else:
                    v.tensor_add(T_loc(kvh), pp, T_loc(kvh))
                    v.tensor_scalar(ccpack[:, kvh * 128:(kvh + 1) * 128],
                                    T_loc(kvh), sendf[:, kvh:kvh + 1], None, ALU.mult)
            ve.inc(v.tensor_copy(ccpack[:, 1024 + kvh:1025 + kvh],
                                 B_t(kvh)[:, 1023:1024]))
    ve_ccpack = ve.n

    def issue_qgroup(g, gi):
        h0, gn = QGROUPS[gi]
        for kc in range(NKC):
            s_wq.issue(g, lambda p, kc=kc, h0=h0, gn=gn: [
                g.dma_start(out=wstr(p)[:, 0:gn * 128],
                            in_=wq_d[kc * 128:(kc + 1) * 128,
                                     h0 * 128:(h0 + gn) * 128])])

    @blk.gpsimd
    def _(g):
        g.wait_ge(ve.h, ve_masks)
        ge.inc(g.memset(ones_rf, 1.0))
        g.wait_ge(ve.h, ve_ccpack)
        dGm.inc(g.dma_start(out=cc_in[:, :], in_=ccpack), 16)
        g.wait_ge(dGm.h, dGm.n)
        ccs.inc(g.collective_compute(
            "AllGather", ALU.bypass, replica_groups=[list(range(NCORES))],
            ins=[cc_in.ap().opt()], outs=[cc_out.ap().opt()]))
        for gi in range(len(QGROUPS)):
            issue_qgroup(g, gi)
        g.wait_ge(ccs.h, ccs.n)
        for j in range(NCORES - 1, -1, -1):
            s_gth.issue(g, lambda p, j=j: [
                g.dma_start(out=gth(p), in_=cc_out[j * 128:(j + 1) * 128, :])])
        for kc in range(NKC):
            s_wt1.issue(g, lambda p, kc=kc: [
                g.dma_start(out=wtf(p), in_=wt_d[kc * 128:(kc + 1) * 128, :])])

    ve_cmb_base = ve.n
    ve_cmb = {j: ve_cmb_base + 2 * (NCORES - 1 - j) + 1
              for j in range(NCORES - 1, -1, -1)}
    se_cmb = {j: se.n + (NCORES - 1 - j) for j in range(NCORES - 1)}
    pq_cv = PQ()

    @blk.vector
    def _(v):
        for j in range(NCORES - 1, -1, -1):
            p = s_gth.wait(v)
            if j < NCORES - 1:
                pq_cv.put(v.scalar_tensor_tensor(acc8[:, :], bsave[:, :],
                                                 um_s[:, j + 1:j + 2], acc8[:, :],
                                                 ALU.mult, ALU.add))
            inst = v.tensor_copy(bsave[:, :], gth(p)[:, 1024:1032])
            assert ve.inc(inst) == ve_cmb[j]
            if j < NCORES - 1:
                v.wait_ge(se.h, se_cmb[j])
            for kvh in range(NKV):
                wcol = um_s[:, j:j + 1] if j == NCORES - 1 else w8[:, kvh:kvh + 1]
                inst = v.scalar_tensor_tensor(
                    S_init(kvh), gth(p)[:, kvh * 128:(kvh + 1) * 128],
                    wcol, S_init(kvh), ALU.mult, ALU.add)
            ve.inc(inst)
            if j > 0:
                pq_cv.defer(s_gth.done)
            else:
                s_gth.done_skip()
        assert pq_cv.empty()
    ve_comb = ve.n

    @blk.tensor
    def _(t):
        t.wait_ge(ve.h, ve_ccpack)
        for gi, (h0, gn) in enumerate(QGROUPS):
            r_qj.prod_wait(t)
            for kc in range(NKC):
                p = s_wq.wait(t)
                for hh in range(gn):
                    for half in range(2):
                        inst = t.matmul(pview(hh * 2 + half),
                                        wstr(p)[:, hh * 128:(hh + 1) * 128],
                                        x_bf(kc)[:, half * 512:(half + 1) * 512],
                                        start=(kc == 0), stop=(kc == NKC - 1))
                        if (kc, hh, half) == (0, 0, 0):
                            pq_t.put(inst)
                if kc < NKC - 1:
                    s_wq.done(inst)
            r_qj.prod_done(inst)
            if gi < len(QGROUPS) - 1:
                pq_t.defer(s_wq.done)
            else:
                s_wq.done_skip()

    @blk.scalar
    def _(s):
        for gi, (h0, gn) in enumerate(QGROUPS):
            r_qj.cons_wait(s)
            for hh in range(gn):
                qh = h0 + hh
                for half in range(2):
                    inst = s.activation(q_bf(qh)[:, half * 512:(half + 1) * 512],
                                        pview(hh * 2 + half), AF.Relu,
                                        bias=bq_s[:, qh:qh + 1], scale=1.0)
            r_qj.cons_done(inst)

    @blk.scalar
    def _(s):
        for j in range(NCORES - 2, -1, -1):
            s.wait_ge(ve.h, ve_cmb[j])
            s.activation(w8[:, :], acc8[:, :], AF.Exp, bias=cb[:, 0:1], scale=S16)
            assert se.inc(s.mul(w8[:, :], w8[:, :], um_s[:, j:j + 1])) == se_cmb[j]


    @blk.vector
    def _(v):
        for kc in range(NKC):
            p = s_wt1.wait(v)
            inst = v.reduce_sum(wsum[:, kc:kc + 1], wtf(p), AX.X,
                                apply_absolute_value=True)
            s_wt1.done(inst)
        v.tensor_copy(rscr, rscr)
        ve.inc(v.reduce_sum(wsum[:, 16:17], wsum[:, 0:16], AX.X))
    ve_wsum = ve.n

    @blk.tensor
    def _(t):
        t.wait_ge(ve.h, ve_wsum)
        tns.inc(t.matmul(psA[0:1, 6 * 512:6 * 512 + 1], ones_cf[:, :],
                         wsum[:, 16:17], start=True, stop=True))
    tns_alpha = tns.n

    @blk.vector
    def _(v):
        pad = lambda: v.tensor_copy(rscr, rscr)
        v.wait_ge(tns.h, tns_alpha)
        v.tensor_copy(a11[0:1, 3:4], psA[0:1, 6 * 512:6 * 512 + 1])
        pad()
        v.tensor_scalar(a11[0:1, 0:1], a11[0:1, 3:4],
                        2.0 ** -22, 1e-8, ALU.mult, ALU.max)
        pad()
        v.reciprocal(a11[0:1, 1:2], a11[0:1, 0:1])
        pad()
        v.tensor_mul(a11[0:1, 2:3], a11[0:1, 0:1], a11[0:1, 1:2])
        pad()
        v.tensor_scalar(a11[0:1, 2:3], a11[0:1, 2:3], -1.0, 2.0, ALU.mult, ALU.add)
        pad()
        v.tensor_mul(a11[0:1, 1:2], a11[0:1, 1:2], a11[0:1, 2:3])
        pad()
        ve.inc(v.tensor_scalar(a11[0:1, 2:3], a11[0:1, 1:2], -1.0, None, ALU.mult))
    ve_a11 = ve.n

    @blk.tensor
    def _(t):
        t.wait_ge(ve.h, ve_a11)
        t.wait_ge(ge.h, ge.n)
        t.matmul(psA[:, 7 * 512:7 * 512 + 1], ones_rf[0:1, :], a11[0:1, 2:3],
                 start=True, stop=False)
        tns.inc(t.matmul(psA[:, 7 * 512 + 1:7 * 512 + 2], ones_rf[0:1, :],
                         a11[0:1, 0:1], start=False, stop=True))
    tns_ab = tns.n

    @blk.vector
    def _(v):
        v.wait_ge(tns.h, tns_ab)
        ve.inc(v.tensor_copy(alpha_c[:, 0:2], psA[:, 7 * 512:7 * 512 + 2]))
    ge_alpha = None
    ve_alpha = ve.n

    r_O.prefill(NKV * NCH * 2)
    r_A.prefill(NKV * NCH * 2)
    r_stp.prefill(NKV * NCH)

    @blk.scalar
    def _(s):
        for kvh in range(NKV):
            for c in range(NCH):
                p = r_ebem.prod_wait(s)
                r_ebem.prod_done(
                    s.activation(eb_f(p), B_t(kvh)[:, c * 128:(c + 1) * 128], AF.Exp,
                                 bias=bqt[:, kvh * 8 + c:kvh * 8 + c + 1], scale=S16))

    @blk.vector
    def _(v):
        for kvh in range(NKV):
            for c in range(NCH):
                tsl = r_T.prod_wait(v)
                r_T.prod_done(v.scalar_tensor_tensor(
                    ttot_b(tsl), S_init(kvh), emid[:, kvh * 8 + c:kvh * 8 + c + 1],
                    T_bf(kvh, c), ALU.mult, ALU.add))
                ep = r_ebem.cons_wait(v)
                for qi in range(2):
                    qh = 2 * kvh + qi
                    qsl = r_qt.prod_wait(v)
                    r_qt.prod_done(v.tensor_mul(
                        qt_b(qsl), q_bf(qh)[:, c * 128:(c + 1) * 128], eb_f(ep)))
                    asl = r_A.cons_wait(v)
                    bsl = r_Ab.prod_wait(v)
                    r_Ab.prod_done(v.tensor_mul(A_b(bsl), pview(6 + asl, 128),
                                                maskU[:, :]))
                    osl = rO_ve.wait(v)
                    oslice = o_bf(qh)[:, c * 128:(c + 1) * 128]
                    cp = v.tensor_copy(oslice, pview(4 + osl, 128))
                    if qi == 1:
                        r_ebem.cons_done(cp)
                    ssl = r_sq.prod_wait(v)
                    r_sq.prod_done(v.tensor_mul(osq_b(ssl), oslice, oslice))
                stsl = r_stp.cons_wait(v)
                inst = v.tensor_add(stat_sb[:, c * 128:(c + 1) * 128],
                                    stat_sb[:, c * 128:(c + 1) * 128],
                                    pview(2 + stsl, 128)[0:16, :])
        ve.inc(inst)
    ve_oside = ve.n

    @blk.tensor
    def _(t):
        for kvh in range(NKV):
            for c in range(NCH):
                tsl = r_T.cons_wait(t)
                stsl = r_stp.prod_wait(t)
                for qi in range(2):
                    qh = 2 * kvh + qi
                    qsl = r_qt.cons_wait(t)
                    asl = r_A.prod_wait(t)
                    r_A.prod_done(t.matmul(pview(6 + asl, 128),
                                           kt_t(kvh)[:, c * 128:(c + 1) * 128],
                                           qt_b(qsl), start=True, stop=True))
                    bsl = r_Ab.cons_wait(t)
                    osl = r_O.prod_wait(t)
                    t.matmul(pview(4 + osl, 128),
                             v_tm(c)[:, kvh * 128:(kvh + 1) * 128], A_b(bsl),
                             start=True, stop=False)
                    r_O.prod_done(t.matmul(pview(4 + osl, 128), ttot_b(tsl),
                                           qt_b(qsl), start=False, stop=True))
                    ssl = r_sq.cons_wait(t)
                    inst = t.matmul(pview(2 + stsl, 128)[0:16, :],
                                    sel[:, qh * 16:(qh + 1) * 16], osq_b(ssl),
                                    start=(qi == 0), stop=(qi == 1))
                    if qi == 1:
                        r_stp.prod_done(inst)

    @blk.vector
    def _(v):
        v.wait_ge(ve.h, ve_oside)
        v.tensor_scalar(stat_sb, stat_sb, 1.0 / 128.0, 1e-5, ALU.mult, ALU.add)
        ve.inc(v.reciprocal(r_sb, stat_sb))
    ve_gn1 = ve.n

    @blk.scalar
    def _(s):
        s.wait_ge(ve.h, ve_gn1)
        se.inc(s.activation(r_sb, r_sb, AF.Sqrt, bias=cb[0:16, 0:1], scale=1.0))
    se_gn = se.n

    @blk.vector
    def _(v):
        v.wait_ge(se.h, se_gn)
        v.tensor_mul(rscr, r_sb, r_sb)
        v.tensor_mul(rscr, rscr, stat_sb)
        v.tensor_scalar(rscr, rscr, -0.5, 1.5, ALU.mult, ALU.add)
        v.tensor_mul(r_sb, r_sb, rscr)
        v.tensor_copy(rhl[:, 0:1024], r_sb)
        v.tensor_sub(rscr, r_sb, rhl[:, 0:1024])
        ve.inc(v.tensor_copy(rhl[:, 1024:2048], rscr))
    ve_rhl = ve.n

    @blk.sync
    def _(sy):
        sy.wait_ge(ve.h, ve_rhl)
        for rnd in range(6):
            h0 = rnd * 3
            nh = min(3, 16 - h0)
            s_rst.issue(sy, lambda p, h0=h0, nh=nh: [
                sy.dma_start(out=rstage(p)[0:32 * (nh - 1) + 1:32, :],
                             in_=rhl[h0:h0 + nh, :])])

    @blk.tensor
    def _(t):
        t.wait_ge(ve.h, ve_oside)
        for rnd in range(6):
            h0 = rnd * 3
            nh = min(3, 16 - h0)
            p = s_rst.wait(t)
            for hr in range(nh):
                base = 32 * hr
                rsl = r_rb.prod_wait(t)
                for half in range(2):
                    pv = pview(rsl * 2 + half, 512)
                    inst0 = t.matmul(pv, ones_b3[base:base + 1, :],
                                     rstage(p)[base:base + 1,
                                               half * 512:(half + 1) * 512],
                                     start=True, stop=False)
                    if (hr, half) == (0, 0):
                        pq_t.put(inst0)
                    inst = t.matmul(pv, ones_b3[base:base + 1, :],
                                    rstage(p)[base:base + 1,
                                              1024 + half * 512:1024 + (half + 1) * 512],
                                    start=False, stop=True)
                r_rb.prod_done(inst)
            if rnd < 4:
                pq_t.defer(s_rst.done)
            else:
                s_rst.done_skip()
        assert pq_t.empty()

    @blk.vector
    def _(v):
        for qh in range(NQH):
            rsl = r_rb.cons_wait(v)
            inst = v.scalar_tensor_tensor(o_bf(qh), psA[:, rsl * 1024:(rsl + 1) * 1024],
                                          gnw_s[:, 0:1], o_bf(qh), ALU.mult, ALU.mult)
            if qh < NQH - 1:
                r_rb.cons_done(inst)
            else:
                ve.inc(inst)
                pq_v.defer(r_rb.cons_done)
    ve_gnorm = ve.n

    @blk.gpsimd
    def _(g):
        first = [True]
        for fh in range(4):
            for kcg in range(4):
                if first[0] and (fh, kcg) == (0, 1):
                    g.wait_ge(ve.h, ve_oside)
                    first[0] = False
                s_wo.issue(g, lambda p, fh=fh, kcg=kcg: [
                    g.dma_start(out=wostr(p)[:, i * 512:(i + 1) * 512],
                                in_=wo_d[(kcg * 4 + i) * 128:(kcg * 4 + i + 1) * 128,
                                         fh * 512:(fh + 1) * 512])
                    for i in range(4)])

    @blk.tensor
    def _(t):
        t.wait_ge(ve.h, ve_gnorm)
        for fh in range(4):
            r_wop.prod_wait(t)
            for kcg in range(4):
                p = s_wo.wait(t)
                for i in range(4):
                    kc = kcg * 4 + i
                    for ts in range(NTS):
                        inst = t.matmul(pview(ts), o_bf(kc)[:, ts * 128:(ts + 1) * 128],
                                        wostr(p)[:, i * 512:(i + 1) * 512],
                                        start=(kc == 0), stop=(kc == NKC - 1))
                        if (kcg, i, ts) == (0, 0, 0):
                            pq_t.put(inst)
                if kcg < 3:
                    s_wo.done(inst)
            r_wop.prod_done(inst)
            if fh < 3:
                pq_t.defer(s_wo.done)
            else:
                s_wo.done_skip()
        assert pq_t.empty()
    wo_done_tgt = lambda: r_wop.tgts[3]

    ve_wo_ev = {}

    @blk.vector
    def _(v):
        for fh in range(4):
            r_wop.cons_wait(v)
            for ts in range(NTS):
                inst = v.tensor_copy(xr(ts)[:, fh * 512:(fh + 1) * 512], pview(ts))
                if fh == 3:
                    ve_wo_ev[ts] = ve.inc(inst)
                elif ts == NTS - 1:
                    r_wop.cons_done(inst)
                else:
                    pq_v.put(inst)
        assert pq_v.empty()

    @blk.gpsimd
    def _(g):
        for ts in range(NTS):
            g.wait_ge(ve.h, ve_wo_ev[ts])
            dAcc.inc(g.dma_start(out=xr(ts), in_=xtm[ts * 128:(ts + 1) * 128, :],
                                 accum_op=ALU.add), 16)
        g.wait_ge(r_rb.rdy.h, r_rb.tgts[-1])
        for kc in range(NKC):
            s_wt2.issue(g, lambda p, kc=kc: [
                g.dma_start(out=wtf(p), in_=wt_d[kc * 128:(kc + 1) * 128, :])])

    Lc = lambda ts: lncol[:, ts * 4:ts * 4 + 4]
    B2c = lambda ts: bn2[:, ts * 2:ts * 2 + 2]

    def ln_batch_a(v, wait_fn):
        for ts in range(NTS):
            wait_fn(v, ts)
            for qq in range(4):
                v.bn_stats(bn6[:, ts * 24 + qq * 6:ts * 24 + (qq + 1) * 6],
                           xr(ts)[:, qq * 512:(qq + 1) * 512])
        for ts in range(NTS):
            v.bn_aggr(B2c(ts), bn6[:, ts * 24:ts * 24 + 24])
        for ts in range(NTS):
            v.tensor_scalar(Lc(ts)[:, 0:1], B2c(ts)[:, 1:2], 1e-5, None, ALU.add)
        for ts in range(NTS):
            ve.inc(v.reciprocal(Lc(ts)[:, 1:2], Lc(ts)[:, 0:1]))

    def ln_batch_b(v, wait_fn):
        for ts in range(NTS):
            wait_fn(v, ts)
            v.tensor_mul(Lc(ts)[:, 2:3], Lc(ts)[:, 1:2], Lc(ts)[:, 1:2])
        for ts in range(NTS):
            v.tensor_mul(Lc(ts)[:, 2:3], Lc(ts)[:, 2:3], Lc(ts)[:, 0:1])
        for ts in range(NTS):
            v.tensor_scalar(Lc(ts)[:, 2:3], Lc(ts)[:, 2:3], -0.5, 1.5,
                            ALU.mult, ALU.add)
        for ts in range(NTS):
            v.tensor_mul(Lc(ts)[:, 1:2], Lc(ts)[:, 1:2], Lc(ts)[:, 2:3])
        for ts in range(NTS):
            ve.inc(v.scalar_tensor_tensor(Lc(ts)[:, 2:3], B2c(ts)[:, 0:1], -1.0,
                                          Lc(ts)[:, 1:2], ALU.mult, ALU.mult))

    @blk.vector
    def _(v):
        ln_batch_a(v, lambda vv, ts: vv.wait_ge(dAcc.h, 16 * (ts + 1)))
    ve_ln1a0 = ve.n - NTS

    @blk.scalar
    def _(s):
        for ts in range(NTS):
            s.wait_ge(ve.h, ve_ln1a0 + ts + 1)
            L = lncol[:, ts * 4:ts * 4 + 4]
            se.inc(s.activation(L[:, 1:2], L[:, 1:2], AF.Sqrt,
                                bias=cb[:, 0:1], scale=1.0))
    se_ln1s0 = se.n - NTS

    @blk.vector
    def _(v):
        ln_batch_b(v, lambda vv, ts: vv.wait_ge(se.h, se_ln1s0 + ts + 1))
    ve_ln1b0 = ve.n - NTS

    se_x2 = {}

    @blk.scalar
    def _(s):
        for ts in range(NTS):
            s.wait_ge(ve.h, ve_ln1b0 + ts + 1)
            if dbg_x2:
                s.wait_ge(dTap.h, 16 * (10 + ts))
            L = lncol[:, ts * 4:ts * 4 + 4]
            se_x2[ts] = se.inc(s.activation(xr(ts), xr(ts), AF.Identity,
                                            bias=L[:, 2:3], scale=L[:, 1:2]))

    if dbg_x2:
        @blk.sync
        def _(sy):
            sy.wait_ge(ve.h, ve_oside)
            dTap.inc(sy.dma_start(out=tapT7[:, :], in_=AR7[:, :]), 16)
            dTap.inc(sy.dma_start(out=tapB[:, :], in_=AR3[:, 0:1024]), 16)
            for nm, off, w in [(mtab, 0, 64), (ftab, 64, 64), (bkt, 128, 64),
                               (bqt, 192, 64), (emid, 256, 64), (sendf, 320, 8)]:
                dTap.inc(sy.dma_start(out=tapTab[:, off:off + w], in_=nm[:, :]), 16)
            sy.wait_ge(ve.h, ve_gnorm)
            dTap.inc(sy.dma_start(out=tapo[:, :], in_=AR1[:, :]), 16)
            for ts in range(NTS):
                sy.wait_ge(dAcc.h, 16 * (ts + 1))
                dTap.inc(sy.dma_start(out=tapxr[ts * 128:(ts + 1) * 128, :],
                                      in_=xr(ts)), 16)
            for ts in range(NTS):
                sy.wait_ge(se.h, se_x2[ts])
                dTap.inc(sy.dma_start(out=tapx2[ts * 128:(ts + 1) * 128, :],
                                      in_=xr(ts)), 16)



    @blk.scalar
    def _(s):
        s.wait_ge(ve.h, ve_alpha)
        s.wait_ge(ve.h, ve_oside)
        if dbg_x2:
            s.wait_ge(dTap.h, 16 * 8)
        for kc in range(NKC):
            p = s_wt2.wait(s)
            for hq in range(4):
                uq = r_uq.prod_wait(s)
                inst = s.activation(uw_f32(uq), wtf(p)[:, hq * 512:(hq + 1) * 512],
                                    AF.Identity, bias=cb[:, 0:1],
                                    scale=alpha_c[:, 0:1])
                r_uq.prod_done(inst)

    @blk.vector
    def _(v):
        v.wait_ge(r_wop.rdy.h, wo_done_tgt())
        for kc in range(NKC):
            for hq in range(4):
                uq = r_uq.cons_wait(v)
                cp = v.tensor_copy(iw_i32(uq), uw_f32(uq))
                if hq == 3 and kc < NKC - 1:
                    s_wt2.done(cp)
                inst = v.tensor_scalar(wtq(kc)[:, hq * 512:(hq + 1) * 512],
                                       iw_i32(uq), -1.0, 1.0, ALU.max, ALU.min)
                if (kc, hq) == (NKC - 1, 3):
                    ve.inc(inst)
                else:
                    r_uq.cons_done(inst)
        s_wt2.done_skip()
    ve_wq = ve.n

    ve_bq0 = {}

    @blk.vector
    def _(v):
        for ts in range(NTS):
            v.wait_ge(se.h, se_x2[ts])
            v.reduce_max(abcols[:, 2 * ts:2 * ts + 1], xr(ts), AX.X,
                         apply_absolute_value=True)
        for ts in range(NTS):
            v.tensor_scalar(abcols[:, 2 * ts:2 * ts + 1],
                            abcols[:, 2 * ts:2 * ts + 1], 1.0 / 127.0, 1e-8,
                            ALU.mult, ALU.max)
        for ts in range(NTS):
            v.tensor_mul(abcols[:, 2 * ts + 1:2 * ts + 2],
                         abcols[:, 2 * ts:2 * ts + 1], alpha_c[:, 1:2])
        for ts in range(NTS):
            v.reciprocal(abcols[:, 2 * ts:2 * ts + 1], abcols[:, 2 * ts:2 * ts + 1])
        for ts in range(NTS):
            ve_bq0[ts] = ve.inc(
                v.tensor_scalar(abcols[:, 2 * ts:2 * ts + 1],
                                abcols[:, 2 * ts:2 * ts + 1], -1.0, None, ALU.mult))

    if dbg_x2:
        @blk.sync
        def _(sy):
            sy.wait_ge(ve.h, ve_bq0[NTS - 1])
            dTap.inc(sy.dma_start(out=tapAB[:, 0:16], in_=abcols[:, :]), 16)
            dTap.inc(sy.dma_start(out=tapAB[:, 16:18], in_=alpha_c[:, :]), 16)
            dTap.inc(sy.dma_start(out=tapAB[0:1, 18:22], in_=a11[:, :]), 16)

    @blk.scalar
    def _(s):
        s.wait_ge(r_wop.rdy.h, wo_done_tgt())
        if dbg_x2:
            s.wait_ge(dTap.h, 16 * 9)
        for ts in range(NTS):
            s.wait_ge(ve.h, ve_bq0[ts])
            for hq in range(2):
                uq = r_ux.prod_wait(s)
                r_ux.prod_done(
                    s.activation(ux_f32(uq), xr(ts)[:, hq * 1024:(hq + 1) * 1024],
                                 AF.Identity, bias=cb[:, 0:1],
                                 scale=abcols[:, 2 * ts:2 * ts + 1]))

    dT_x0 = dT.n

    @blk.vector
    def _(v):
        for ts in range(NTS):
            if ts >= 2:
                v.wait_ge(dT.h, dT_x0 + 256 * (ts - 1))
            xsl = r_xq.prod_wait(v)
            for hq in range(2):
                uq = r_ux.cons_wait(v)
                cp = v.tensor_copy(ix_i32(uq), ux_f32(uq))
                pq_v.put(cp)
                inst = v.tensor_scalar(x2q_tm(xsl)[:, hq * 1024:(hq + 1) * 1024],
                                       ix_i32(uq), -127.0, 127.0, ALU.max, ALU.min)
                if hq == 1:
                    r_xq.prod_done(inst)
                    pq_v.defer(r_ux.cons_done)
                else:
                    r_ux.cons_done(inst)
        pq_v.q.pop()
        assert pq_v.empty()

    @blk.sync
    def _(sy):
        sy.wait_ge(ve.h, ve_wq)
        sy.wait_ge(r_wop.rdy.h, wo_done_tgt())
        for ts in range(NTS):
            xsl = r_xq.cons_wait(sy)
            for kc in range(NKC):
                inst = sy.dma_start_transpose(
                    x2q_fm(kc)[:, ts * 128:(ts + 1) * 128],
                    x2q_tm(xsl)[:, kc * 128:(kc + 1) * 128])
                dT.inc(inst, 16)
    dT_all = dT.n

    @blk.tensor
    def _(t):
        t.wait_ge(dT.h, dT_all)
        t.wait_ge(ve.h, ve_wq)
        if dbg_x2:
            t.wait_ge(dTap.h, dTap.n)
        for og in range(4):
            r_ffp.prod_wait(t)
            for kc in range(NKC):
                for ts in range(NTS):
                    inst = t.matmul(pview(ts),
                                    x2q_fm(kc)[:, ts * 128:(ts + 1) * 128],
                                    wtq(kc)[:, og * 512:(og + 1) * 512],
                                    start=(kc == 0), stop=(kc == NKC - 1))
            r_ffp.prod_done(inst)

    ve_y_ev = {}

    @blk.vector
    def _(v):
        for og in range(4):
            r_ffp.cons_wait(v)
            for ts in range(NTS):
                inst = v.scalar_tensor_tensor(
                    xr(ts)[:, og * 512:(og + 1) * 512], pview(ts),
                    abcols[:, 2 * ts + 1:2 * ts + 2],
                    xr(ts)[:, og * 512:(og + 1) * 512], ALU.mult, ALU.add)
                if og == 3:
                    ve_y_ev[ts] = ve.inc(inst)
                elif ts == NTS - 1:
                    r_ffp.cons_done(inst)

    @blk.vector
    def _(v):
        ln_batch_a(v, lambda vv, ts: vv.wait_ge(ve.h, ve_y_ev[ts]))
    ve_ln2a0 = ve.n - NTS

    @blk.scalar
    def _(s):
        for ts in range(NTS):
            s.wait_ge(ve.h, ve_ln2a0 + ts + 1)
            L = lncol[:, ts * 4:ts * 4 + 4]
            se.inc(s.activation(L[:, 1:2], L[:, 1:2], AF.Sqrt,
                                bias=cb[:, 0:1], scale=1.0))
    se_ln2s0 = se.n - NTS

    @blk.vector
    def _(v):
        ln_batch_b(v, lambda vv, ts: vv.wait_ge(se.h, se_ln2s0 + ts + 1))
    ve_ln2b0 = ve.n - NTS

    @blk.scalar
    def _(s):
        for ts in range(NTS):
            s.wait_ge(ve.h, ve_ln2b0 + ts + 1)
            if ts >= 2:
                s.wait_ge(dO.h, 16 * (ts - 1))
            L = lncol[:, ts * 4:ts * 4 + 4]
            osl = r_ot.prod_wait(s)
            r_ot.prod_done(s.activation(out_stg(osl), xr(ts), AF.Identity,
                                        bias=L[:, 2:3], scale=L[:, 1:2]))

    @blk.sync
    def _(sy):
        for ts in range(NTS):
            osl = r_ot.cons_wait(sy)
            inst = sy.dma_start(out=out_d[ts * 128:(ts + 1) * 128, :],
                                in_=out_stg(osl))
            dO.inc(inst, 16)
        sy.wait_ge(dO.h, dO.n)

    es.close()
    return nc


def kernel(**inputs):
    x = np.ascontiguousarray(np.asarray(inputs["L0_attn_hidden_states"],
                                        dtype=np.float32))[0]
    x_t = np.ascontiguousarray(x.T)
    wq = np.ascontiguousarray(np.asarray(inputs["L0_attn_Wq"], dtype=np.float32))
    wk = np.ascontiguousarray(np.asarray(inputs["L0_attn_Wk"], dtype=np.float32))
    wv = np.ascontiguousarray(np.asarray(inputs["L0_attn_Wv"], dtype=np.float32))
    wg1 = np.ascontiguousarray(np.asarray(inputs["L0_attn_Wg1"], dtype=np.float32))
    wg2 = np.ascontiguousarray(np.asarray(inputs["L0_attn_Wg2"], dtype=np.float32))
    wo = np.ascontiguousarray(np.asarray(inputs["L0_attn_Wo"], dtype=np.float32))
    wt = np.ascontiguousarray(np.asarray(inputs["L0_ffn_weight"], dtype=np.float32).T)
    bq = np.ascontiguousarray(
        np.asarray(inputs["L0_attn_bq"], dtype=np.float32).reshape(16, 128).T)
    bk = np.ascontiguousarray(
        np.asarray(inputs["L0_attn_bk"], dtype=np.float32).reshape(8, 128).T)
    bg2 = np.ascontiguousarray(
        np.asarray(inputs["L0_attn_bg2"], dtype=np.float32).reshape(8, 128).T)
    bv = np.ascontiguousarray(
        np.asarray(inputs["L0_attn_bv"], dtype=np.float32).reshape(1, 1024))
    gnw = np.ascontiguousarray(
        np.asarray(inputs["L0_attn_g_norm_w"], dtype=np.float32).reshape(128, 1))

    nc = _build()
    in_maps = []
    for c in range(NCORES):
        um = np.zeros((128, 8), dtype=np.float32)
        um[:, :c] = 1.0
        in_maps.append({
            "xfm": np.ascontiguousarray(x_t[:, c * T:(c + 1) * T]),
            "xtm": np.ascontiguousarray(x[c * T:(c + 1) * T, :]),
            "wq": wq, "wk": wk, "wv": wv, "wg1": wg1, "wg2": wg2,
            "wo": wo, "wt": wt, "bqr": bq, "bkr": bk, "bg2r": bg2,
            "bvr": bv, "gnw": gnw, "um": um,
        })
    res = run_bass_kernel_spmd(nc, in_maps, core_ids=list(range(NCORES)))
    out = np.concatenate([np.asarray(r["out"]) for r in res.results], axis=0)
    return out.reshape(1, NCORES * T, HID)


# version 16 (final)
# speedup vs baseline: 1.0059x; 1.0059x over previous
import contextlib
import numpy as np

import concourse.bass as bass
import concourse.mybir as mybir
from concourse.bass_utils import run_bass_kernel_spmd

F32 = mybir.dt.float32
BF16 = mybir.dt.bfloat16
I32 = mybir.dt.int32
AF = mybir.ActivationFunctionType
ALU = mybir.AluOpType
AX = mybir.AxisListType

NCORES = 8
HID = 2048
T = 1024
NKC = 16
NKV = 8
NQH = 16
D = 128
NCH = 8
NTS = 8
S16 = 1.0 / 16.0
LNSQD = float(np.log(np.sqrt(128.0)))

KGROUPS = [(0, 3), (3, 3), (6, 2)]
QGROUPS = [(0, 3), (3, 3), (6, 3), (9, 3), (12, 3), (15, 1)]
QGRP_OF_KVH = [0, 1, 1, 2, 3, 3, 4, 5]


class _Sem:
    __slots__ = ("h", "n")

    def __init__(self, es, nc, name):
        self.h = es.enter_context(nc.semaphore(name))
        self.n = 0

    def inc(self, inst, k=1):
        self.n += k
        inst.then_inc(self.h, k)
        return self.n


class Ring:

    def __init__(self, es, nc, name, n, cons_per=1, data_only=False):
        self.rdy = _Sem(es, nc, name + "_r")
        self.fre = None if data_only else _Sem(es, nc, name + "_f")
        self.n = n
        self.cons_per = cons_per
        self.ip = 0
        self.tgts = []
        self._ic0 = 0

    def prod_wait(self, eng):
        if self.fre is not None and self.ip >= self.n:
            eng.wait_ge(self.fre.h, self.cons_per * (self.ip - self.n + 1))
        return self.ip % self.n

    def prod_done(self, inst):
        self.rdy.inc(inst)
        if self.ip < len(self.tgts):
            assert self.tgts[self.ip] == self.rdy.n
        else:
            self.tgts.append(self.rdy.n)
        self.ip += 1

    def prefill(self, count):
        self.tgts = self.tgts + [self.rdy.n + i + 1 for i in range(count)]

    def cons_wait(self, eng):
        slot = self._ic0 % self.n
        eng.wait_ge(self.rdy.h, self.tgts[self._ic0])
        self._ic0 += 1
        return slot

    def cons_done(self, inst):
        self.fre.inc(inst)

    def consumer(self):
        ring = self

        class _C:
            def __init__(self):
                self.ic = 0

            def wait(self, eng):
                slot = self.ic % ring.n
                eng.wait_ge(ring.rdy.h, ring.tgts[self.ic])
                self.ic += 1
                return slot

        return _C()


class TileStream:
    def __init__(self, es, nc, name, nbuf=2):
        self.ld = [_Sem(es, nc, f"{name}_l{i}") for i in range(nbuf)]
        self.fre = _Sem(es, nc, name + "_f")
        self.nbuf = nbuf
        self.ti = 0
        self.tc = 0
        self.tgts = []

    def issue(self, eng, emit):
        t = self.ti
        p = t % self.nbuf
        if t >= self.nbuf:
            eng.wait_ge(self.fre.h, t - self.nbuf + 1)
        for i in emit(p):
            self.ld[p].inc(i, 16)
        self.tgts.append((p, self.ld[p].n))
        self.ti += 1

    def wait(self, eng):
        p, tgt = self.tgts[self.tc]
        eng.wait_ge(self.ld[p].h, tgt)
        self.tc += 1
        return p

    def done(self, inst):
        self.fre.inc(inst)

    def done_skip(self):
        pass


class PQ:

    def __init__(self):
        self.q = []

    def defer(self, fn):
        self.q.append(fn)

    def put(self, inst, fn=None):
        if fn is not None:
            fn(inst)
        elif self.q:
            self.q.pop(0)(inst)
        return inst

    def empty(self):
        return not self.q


def _build(dbg_x2=False):
    nc = bass.Bass("TRN2", target_bir_lowering=False, debug=False,
                   detect_race_conditions=False)

    def din(name, shape, dtype=F32):
        return nc.dram_tensor(name, shape, dtype, kind="ExternalInput")

    xfm = din("xfm", [HID, T])
    xtm = din("xtm", [T, HID])
    wq_d = din("wq", [HID, HID])
    wk_d = din("wk", [HID, NKV * D])
    wv_d = din("wv", [HID, NKV * D])
    wg1_d = din("wg1", [HID, 16])
    wg2_d = din("wg2", [16, NKV * D])
    wo_d = din("wo", [HID, HID])
    wt_d = din("wt", [HID, HID])
    bqr = din("bqr", [128, 16])
    bkr = din("bkr", [128, 8])
    bg2r = din("bg2r", [128, 8])
    bvr = din("bvr", [1, NKV * D])
    gnw_d = din("gnw", [128, 1])
    um_d = din("um", [128, 8])
    out_d = nc.dram_tensor("out", [T, HID], F32, kind="ExternalOutput")
    cc_in = nc.dram_tensor("cc_in", [128, 1032], F32)
    cc_out = nc.dram_tensor("cc_out", [NCORES * 128, 1032], F32)
    tapx2 = (nc.dram_tensor("tapx2", [T, HID], F32, kind="ExternalOutput")
             if dbg_x2 else None)
    tapo = (nc.dram_tensor("tapo", [128, 16384], mybir.dt.bfloat16, kind="ExternalOutput")
            if dbg_x2 else None)
    tapT7 = (nc.dram_tensor("tapT7", [128, 8192], mybir.dt.bfloat16, kind="ExternalOutput")
             if dbg_x2 else None)
    tapTab = (nc.dram_tensor("tapTab", [128, 328], F32, kind="ExternalOutput")
              if dbg_x2 else None)
    tapB = (nc.dram_tensor("tapB", [128, 1024], F32, kind="ExternalOutput")
            if dbg_x2 else None)
    tapxr = (nc.dram_tensor("tapxr", [T, HID], F32, kind="ExternalOutput")
             if dbg_x2 else None)
    tapAB = (nc.dram_tensor("tapAB", [128, 22], F32, kind="ExternalOutput")
             if dbg_x2 else None)

    es = contextlib.ExitStack()
    sbt = lambda name, shape, dtype: es.enter_context(nc.sbuf_tensor(name, shape, dtype))

    iota_f = sbt("iota_f", [128, 384], BF16)
    maskU = sbt("maskU", [128, 128], BF16)
    sel = sbt("sel", [128, 256], BF16)
    ones_b = sbt("ones_b", [1, 128], BF16)
    ones_b3 = sbt("ones_b3", [65, 128], BF16)
    ones_cf = sbt("ones_cf", [128, 1], F32)
    cb = sbt("cb", [128, 2], F32)
    bq_s = sbt("bq_s", [128, 16], F32)
    bk_s = sbt("bk_s", [128, 8], F32)
    bg2_s = sbt("bg2_s", [128, 8], F32)
    bv_b = sbt("bv_b", [1, NKV * D], BF16)
    gnw_s = sbt("gnw_s", [128, 1], F32)
    um_s = sbt("um_s", [128, 8], F32)
    mtab = sbt("mtab", [128, 64], F32)
    ftab = sbt("ftab", [128, 64], F32)
    bkt = sbt("bkt", [128, 64], F32)
    bqt = sbt("bqt", [128, 64], F32)
    emid = sbt("emid", [128, 64], F32)
    sendf = sbt("sendf", [128, 8], F32)
    acc8 = sbt("acc8", [128, 8], F32)
    w8 = sbt("w8", [128, 8], F32)
    bsave = sbt("bsave", [128, 8], F32)
    alpha_c = sbt("alpha_c", [128, 2], F32)
    a11 = sbt("a11", [1, 4], F32)
    wsum = sbt("wsum", [128, 17], F32)
    abcols = sbt("abcols", [128, 16], F32)
    bn2 = sbt("bn2", [128, 16], F32)
    lncol = sbt("lncol", [128, 32], F32)

    AR1 = sbt("AR1", [128, 16384], BF16)
    AR2 = sbt("AR2", [128, 16384], BF16)
    AR3 = sbt("AR3", [128, 8192], F32)
    AR4 = sbt("AR4", [128, 8192], F32)
    AR5 = sbt("AR5", [128, 8192], BF16)
    AR6 = sbt("AR6", [128, 8192], BF16)
    AR7 = sbt("AR7", [128, 8192], BF16)
    AR8 = sbt("AR8", [128, 8192], BF16)
    AR9 = sbt("AR9", [128, 4864], BF16)

    x_bf = lambda kc: AR1[:, kc * 1024:(kc + 1) * 1024]
    o_bf = lambda qh: AR1[:, qh * 1024:(qh + 1) * 1024]
    ux_f32 = lambda p: AR1.bitcast(F32)[:, p * 1024:(p + 1) * 1024]
    ix_i32 = lambda p: AR1.bitcast(I32)[:, 2048 + p * 1024:2048 + (p + 1) * 1024]
    x2q_tm = lambda p: AR1[:, 8192 + p * 2048:8192 + (p + 1) * 2048]
    out_stg = lambda p: AR1.bitcast(F32)[:, 2048 + p * 2048:2048 + (p + 1) * 2048]

    q_bf = lambda qh: AR2[:, qh * 1024:(qh + 1) * 1024]
    wv_r = lambda kc: AR2[:, kc * 1024:(kc + 1) * 1024]

    B_t = lambda kvh: AR3[:, kvh * 1024:(kvh + 1) * 1024]
    k_rot = lambda p: AR4[:, p * 1024:(p + 1) * 1024]
    T_loc = lambda kvh: AR4[:, 3072 + kvh * 128:3072 + (kvh + 1) * 128]
    S_init = lambda kvh: AR4[:, 4096 + kvh * 128:4096 + (kvh + 1) * 128]
    gth = lambda p: AR4[:, 5120 + p * 1032:5120 + (p + 1) * 1032]
    ccpack = gth(0)
    xr = lambda ts: (AR3 if ts < 4 else AR4)[:, (ts % 4) * 2048:(ts % 4 + 1) * 2048]

    kt_t = lambda kvh: AR5[:, kvh * 1024:(kvh + 1) * 1024]
    v_tm = lambda ts: AR6[:, ts * 1024:(ts + 1) * 1024]
    T_bf = lambda kvh, c: AR7[:, (kvh * 8 + c) * 128:(kvh * 8 + c + 1) * 128]

    def wtq(kc):
        if kc < 8:
            return AR2[:, kc * 2048:(kc + 1) * 2048]
        if kc < 12:
            return AR5[:, (kc - 8) * 2048:(kc - 7) * 2048]
        return AR6[:, (kc - 12) * 2048:(kc - 11) * 2048]

    x2q_fm = lambda kc: (AR7 if kc < 8 else AR8)[:, (kc % 8) * 1024:(kc % 8 + 1) * 1024]
    wtf = lambda p: AR8.bitcast(F32)[:, p * 2048:(p + 1) * 2048]
    stat_sb = AR8.bitcast(F32)[0:16, 0:1024]
    r_sb = AR8.bitcast(F32)[0:16, 1024:2048]
    rscr = AR8.bitcast(F32)[0:16, 2048:3072]
    rhl = AR8[0:16, 6144:8192]
    rstage = lambda p: AR8[:, 2048 + p * 2048:4096 + p * 2048]
    uw_f32 = lambda p: AR7.bitcast(F32)[:, p * 512:(p + 1) * 512]
    iw_i32 = lambda p: AR7.bitcast(I32)[:, 1024 + p * 512:1024 + (p + 1) * 512]

    wg1_b = AR9[:, 0:256]
    bn6 = AR9.bitcast(F32)[:, 128:320]
    wstr = lambda p: AR9[:, 256 + p * 512:256 + (p + 1) * 512]
    ktr = lambda p: AR9[:, 1280 + p * 128:1280 + (p + 1) * 128]
    A_b = lambda p: AR9[:, 1536 + p * 128:1536 + (p + 1) * 128]
    osq_b = lambda p: AR9[:, 1792 + p * 128:1792 + (p + 1) * 128]
    qt_b = lambda p: AR9[:, 2048 + p * 128:2048 + (p + 1) * 128]
    ttot_b = ktr
    eb_f = lambda p: AR9.bitcast(F32)[:, 1152 + p * 128:1152 + (p + 1) * 128]
    wg2_b = AR9[0:16, 2816:3840]
    t_bf16 = AR9[0:16, 1280:2304]
    wostr = lambda p: (AR9[:, 2816:4864] if p == 0 else AR7[:, 4096:6144])

    ones_rf = iota_f.bitcast(F32)[0:1, 0:128]
    ident = iota_f[:, 256:384]
    psA = es.enter_context(nc.psum_tensor("psA", [128, 4096], F32))
    pview = lambda b, n=512: psA[:, b * 512:b * 512 + n]
    ps_bf = psA.bitcast(BF16)
    tr_bf = lambda sl: ps_bf[:, (6 + sl) * 1024:(6 + sl) * 1024 + 128]

    sems = {}
    for s in ["dGm", "dAcc", "dT", "dO", "dR", "tns", "ve", "se", "ge", "ccs"]:
        sems[s] = _Sem(es, nc, s)
    dTap = _Sem(es, nc, "dTap") if dbg_x2 else None
    dGm, dAcc, dT, dO, dR = (sems[k] for k in ["dGm", "dAcc", "dT", "dO", "dR"])
    tns, ve, se, ge, ccs = (sems[k] for k in ["tns", "ve", "se", "ge", "ccs"])

    s_wv = TileStream(es, nc, "swv", nbuf=4)
    s_wk = TileStream(es, nc, "swk")
    s_wq = TileStream(es, nc, "swq")
    s_wt1 = TileStream(es, nc, "swt1")
    s_wt2 = TileStream(es, nc, "swt2")
    s_wo = TileStream(es, nc, "swo")
    s_gth = TileStream(es, nc, "sgth")
    s_rst = TileStream(es, nc, "srst")
    r_vp = Ring(es, nc, "rvp", 2)
    r_kj = Ring(es, nc, "rkj", 1)
    r_ebem = Ring(es, nc, "rebem", 2)
    r_tr = Ring(es, nc, "rtr", 2, data_only=True)
    r_kt = Ring(es, nc, "rkt", 2, data_only=True)
    r_p = Ring(es, nc, "rp", 2, data_only=True)
    r_qj = Ring(es, nc, "rqj", 1)
    r_qt = Ring(es, nc, "rqt", 2, data_only=True)
    r_A = Ring(es, nc, "rA", 2, data_only=True)
    r_Ab = Ring(es, nc, "rAb", 2, data_only=True)
    r_T = Ring(es, nc, "rT", 2, data_only=True)
    r_O = Ring(es, nc, "rO", 2, data_only=True)
    rO_ve = r_O.consumer()
    r_sq = Ring(es, nc, "rsq", 2, data_only=True)
    r_stp = Ring(es, nc, "rstp", 2, data_only=True)
    r_rb = Ring(es, nc, "rrb", 2)
    r_uq = Ring(es, nc, "ruq", 2)
    r_wop = Ring(es, nc, "rwop", 1)
    r_ffp = Ring(es, nc, "rffp", 1)
    r_ux = Ring(es, nc, "rux", 2)
    r_xq = Ring(es, nc, "rxq", 2, data_only=True)
    r_ot = Ring(es, nc, "rot", 2, data_only=True)

    blk = es.enter_context(nc.Block())

    @blk.gpsimd
    def _(g):
        g.iota(iota_f[:, 0:128], [[1, 128]], channel_multiplier=-1,
               allow_small_or_imprecise_dtypes=True)
        g.iota(iota_f[:, 128:384], [[-1, 16], [1, 16]], channel_multiplier=0,
               allow_small_or_imprecise_dtypes=True)
        g.memset(ones_b[:, :], 1.0)
        g.memset(ones_b3[:, :], 1.0)
        g.memset(ones_cf[:, :], 1.0)
        g.memset(cb[:, 0:1], 0.0)
        g.memset(cb[:, 1:2], -LNSQD)
        g.memset(AR7[:, :], 0.0)
        g.memset(AR4[:, 4096:5120], 0.0)
        g.memset(acc8[:, :], 0.0)
        ge.inc(g.memset(stat_sb, 0.0))
        dGm.inc(g.dma_start(out=bq_s[:, :], in_=bqr[:, :]), 16)
        dGm.inc(g.dma_start(out=bk_s[:, :], in_=bkr[:, :]), 16)
        dGm.inc(g.dma_start(out=bg2_s[:, :], in_=bg2r[:, :]), 16)
        dGm.inc(g.dma_start(out=bv_b[:, :], in_=bvr[:, :]), 16)
        dGm.inc(g.dma_start(out=gnw_s[:, :], in_=gnw_d[:, :]), 16)
        dGm.inc(g.dma_start(out=um_s[:, :], in_=um_d[:, :]), 16)
        for j in range(NKC):
            dGm.inc(g.dma_start(out=wg1_b[:, j * 16:(j + 1) * 16],
                                in_=wg1_d[j * 128:(j + 1) * 128, :]), 16)
        dGm.inc(g.dma_start(out=wg2_b[:, :], in_=wg2_d[:, :]), 16)
        dGm_xb = dGm.n
        for kc in range(NKC):
            if kc >= 4:
                g.wait_ge(dGm.h, dGm_xb + 16 * (kc - 3))
            dGm.inc(g.dma_start(out=x_bf(kc), in_=xfm[kc * 128:(kc + 1) * 128, :]), 16)

    dGm_init = dGm.n

    @blk.vector
    def _(v):
        v.wait_ge(ge.h, ge.n)
        v.tensor_scalar(maskU[:, :], iota_f[:, 0:128], 0.0, None, ALU.is_ge)
        v.tensor_scalar(sel[:, :], iota_f[:, 128:384], 0.0, None, ALU.is_equal)
        ve.inc(v.tensor_scalar(ident[:, :], iota_f[:, 0:128], 0.0, None, ALU.is_equal))
    ve_masks = ve.n

    @blk.tensor
    def _(t):
        t.wait_ge(dGm.h, dGm_init)
        t.wait_ge(ve.h, ve_masks)
        for h in range(2):
            for kc in range(NKC):
                inst = t.matmul(psA[0:16, h * 512:(h + 1) * 512],
                                wg1_b[:, kc * 16:(kc + 1) * 16],
                                x_bf(kc)[:, h * 512:(h + 1) * 512],
                                start=(kc == 0), stop=(kc == NKC - 1))
        tns.inc(inst)
    tns_t = tns.n


    @blk.scalar
    def _(s):
        s.wait_ge(tns.h, tns_t)
        se.inc(s.copy(t_bf16[:, :], psA[0:16, 0:1024]))
    se_t = se.n


    @blk.tensor
    def _(t):
        t.wait_ge(se.h, se_t)
        for kvh in range(NKV):
            if kvh >= 1:
                t.wait_ge(se.h, se_t + 2 * kvh)
            for h in range(2):
                inst = t.matmul(pview(2 + h), wg2_b[:, kvh * 128:(kvh + 1) * 128],
                                t_bf16[:, h * 512:(h + 1) * 512],
                                start=True, stop=True)
            tns.inc(inst)
    tns_g0 = tns.n - NKV


    @blk.scalar
    def _(s):
        for kvh in range(NKV):
            s.wait_ge(tns.h, tns_g0 + 1 + kvh)
            if kvh >= 2:
                s.wait_ge(ve.h, ve_masks + 4 * (kvh - 1))
            ls = k_rot(kvh % 2)
            se.inc(s.activation(ls, psA[:, 1024:2048], AF.Sigmoid,
                                bias=bg2_s[:, kvh:kvh + 1], scale=1.0))
            se.inc(s.activation(ls, ls, AF.Ln, bias=cb[:, 0:1], scale=1.0))

    @blk.vector
    def _(v):
        for kvh in range(NKV):
            v.wait_ge(se.h, se_t + 2 * (kvh + 1))
            ve.inc(v.tensor_tensor_scan(B_t(kvh), k_rot(kvh % 2), k_rot(kvh % 2),
                                        0.0, ALU.add, ALU.bypass))
            m8 = mtab[:, kvh * 8:(kvh + 1) * 8]
            ve.inc(v.tensor_copy(m8, B_t(kvh)[:, 63:1024:128]))
            ve.inc(v.tensor_sub(ftab[:, kvh * 8 + 1:kvh * 8 + 8],
                                B_t(kvh)[:, 191:1024:128],
                                B_t(kvh)[:, 63:896:128]))
            ve.inc(v.tensor_sub(sendf[:, kvh:kvh + 1],
                                B_t(kvh)[:, 1023:1024], B_t(kvh)[:, 959:960]))
    ve_B0 = ve.n - 4 * NKV

    @blk.scalar
    def _(s):
        for kvh in range(NKV):
            s.wait_ge(ve.h, ve_B0 + 4 * (kvh + 1))
            c0, c1 = kvh * 8, (kvh + 1) * 8
            s.activation(ftab[:, c0 + 1:c1], ftab[:, c0 + 1:c1], AF.Exp,
                         bias=cb[:, 0:1], scale=S16)
            s.activation(sendf[:, kvh:kvh + 1], sendf[:, kvh:kvh + 1], AF.Exp,
                         bias=cb[:, 0:1], scale=S16)
            s.activation(bkt[:, c0:c1], mtab[:, c0:c1], AF.Identity,
                         bias=cb[:, 0:1], scale=S16)
            s.activation(bqt[:, c0:c1], mtab[:, c0:c1], AF.Identity,
                         bias=cb[:, 1:2], scale=-S16)
            se.inc(s.activation(emid[:, c0:c1], mtab[:, c0:c1], AF.Exp,
                                bias=cb[:, 0:1], scale=S16))
    se_tab0 = se.n - NKV

    @blk.gpsimd
    def _(g):
        for kc in range(0, NKC, 4):
            if kc >= 4:
                p_prev, tgt_prev = s_wv.tgts[kc // 4 - 1]
                g.wait_ge(s_wv.ld[p_prev].h, tgt_prev)
            s_wv.issue(g, lambda p, kc=kc: [
                g.dma_start(out=wv_r(kc + i),
                            in_=wv_d[(kc + i) * 128:(kc + i + 1) * 128, :])
                for i in range(4)])

    @blk.tensor
    def _(t):
        for _ in range(4):
            s_wv.wait(t)
        for ts in range(NTS):
            slot = r_vp.prod_wait(t)
            b0 = 4 + slot * 2
            for kc in range(NKC):
                for h in range(2):
                    t.matmul(pview(b0 + h), x_bf(kc)[:, ts * 128:(ts + 1) * 128],
                             wv_r(kc)[:, h * 512:(h + 1) * 512],
                             start=(kc == 0), stop=False)
            for h in range(2):
                inst = t.matmul(pview(b0 + h), ones_b[0:1, :],
                                bv_b[0:1, h * 512:(h + 1) * 512],
                                start=False, stop=True)
            r_vp.prod_done(inst)
        for _ in range(4):
            s_wv.done_skip()

    @blk.vector
    def _(v):
        for ts in range(NTS):
            slot = r_vp.cons_wait(v)
            b0 = 4 + slot * 2
            inst = v.tensor_copy(v_tm(ts), psA[:, b0 * 512:(b0 + 2) * 512])
            r_vp.cons_done(inst)

    @blk.gpsimd
    def _(g):
        for gi, (h0, gn) in enumerate(KGROUPS):
            for kc in range(NKC):
                s_wk.issue(g, lambda p, kc=kc, h0=h0, gn=gn: [
                    g.dma_start(out=wstr(p)[:, 0:gn * 128],
                                in_=wk_d[kc * 128:(kc + 1) * 128,
                                         h0 * 128:(h0 + gn) * 128])])

    pq_t = PQ()
    pq_v = PQ()

    @blk.tensor
    def _(t):
        t.wait_ge(r_vp.fre.h, NTS)
        t.wait_ge(se.h, se_t + 2 * NKV)
        for gi, (h0, gn) in enumerate(KGROUPS):
            r_kj.prod_wait(t)
            for kc in range(NKC):
                p = s_wk.wait(t)
                for hh in range(gn):
                    for half in range(2):
                        inst = t.matmul(pview(hh * 2 + half),
                                        wstr(p)[:, hh * 128:(hh + 1) * 128],
                                        x_bf(kc)[:, half * 512:(half + 1) * 512],
                                        start=(kc == 0), stop=(kc == NKC - 1))
                        if (kc, hh, half) == (0, 0, 0):
                            pq_t.put(inst)
                if kc < NKC - 1:
                    s_wk.done(inst)
            r_kj.prod_done(inst)
            if gi < len(KGROUPS) - 1:
                pq_t.defer(s_wk.done)
            else:
                s_wk.done_skip()

    @blk.scalar
    def _(s):
        for gi, (h0, gn) in enumerate(KGROUPS):
            r_kj.cons_wait(s)
            for hh in range(gn):
                kvh = h0 + hh
                ks = kvh % 3
                if kvh >= 3:
                    s.wait_ge(r_ebem.fre.h, 8 * (kvh - 2))
                for half in range(2):
                    inst = s.activation(k_rot(ks)[:, half * 512:(half + 1) * 512],
                                        pview(hh * 2 + half), AF.Relu,
                                        bias=bk_s[:, kvh:kvh + 1], scale=1.0)
            r_kj.cons_done(inst)
            for hh in range(gn):
                kvh = h0 + hh
                s.wait_ge(se.h, se_tab0 + kvh + 1)
                for c in range(NCH):
                    p = r_ebem.prod_wait(s)
                    r_ebem.prod_done(
                        s.activation(eb_f(p), B_t(kvh)[:, c * 128:(c + 1) * 128],
                                     AF.Exp,
                                     bias=bkt[:, kvh * 8 + c:kvh * 8 + c + 1],
                                     scale=-S16))

    @blk.vector
    def _(v):
        for gi, (h0, gn) in enumerate(KGROUPS):
            for hh in range(gn):
                kvh = h0 + hh
                ks = kvh % 3
                for c in range(NCH):
                    p = r_ebem.cons_wait(v)
                    inst = v.tensor_mul(kt_t(kvh)[:, c * 128:(c + 1) * 128],
                                        k_rot(ks)[:, c * 128:(c + 1) * 128], eb_f(p))
                    r_ebem.cons_done(inst)
    ktilde_fre = lambda kvh: 8 * (kvh + 1)

    r_kt.prefill(NKV * NCH)

    @blk.tensor
    def _(t):
        for kvh in range(NKV):
            t.wait_ge(r_ebem.fre.h, ktilde_fre(kvh))
            for c in range(NCH):
                sl = r_tr.prod_wait(t)
                r_tr.prod_done(t.transpose(tr_bf(sl),
                                           kt_t(kvh)[:, c * 128:(c + 1) * 128],
                                           ident[:, :]))
                ksl = r_kt.cons_wait(t)
                psl = r_p.prod_wait(t)
                r_p.prod_done(t.matmul(pview(4 + psl, 128), ktr(ksl),
                                       v_tm(c)[:, kvh * 128:(kvh + 1) * 128],
                                       start=True, stop=True))

    @blk.vector
    def _(v):
        for kvh in range(NKV):
            for c in range(NCH):
                sl = r_tr.cons_wait(v)
                ksl = r_kt.prod_wait(v)
                r_kt.prod_done(v.tensor_copy(ktr(ksl), tr_bf(sl)))
                psl = r_p.cons_wait(v)
                pp = pview(4 + psl, 128)
                if c == 0:
                    v.tensor_copy(T_loc(kvh), pp)
                    v.tensor_scalar(T_loc(kvh), T_loc(kvh),
                                    ftab[:, kvh * 8 + 1:kvh * 8 + 2],
                                    None, ALU.mult)
                    v.tensor_copy(T_bf(kvh, 1), T_loc(kvh))
                elif c < NCH - 1:
                    v.tensor_add(T_loc(kvh), pp, T_loc(kvh))
                    v.tensor_scalar(T_loc(kvh), T_loc(kvh),
                                    ftab[:, kvh * 8 + c + 1:kvh * 8 + c + 2],
                                    None, ALU.mult)
                    v.tensor_copy(T_bf(kvh, c + 1), T_loc(kvh))
                else:
                    v.tensor_add(T_loc(kvh), pp, T_loc(kvh))
                    v.tensor_scalar(ccpack[:, kvh * 128:(kvh + 1) * 128],
                                    T_loc(kvh), sendf[:, kvh:kvh + 1], None, ALU.mult)
            ve.inc(v.tensor_copy(ccpack[:, 1024 + kvh:1025 + kvh],
                                 B_t(kvh)[:, 1023:1024]))
    ve_ccpack = ve.n

    @blk.gpsimd
    def _(g):
        g.wait_ge(ve.h, ve_masks)
        ge.inc(g.memset(ones_rf, 1.0))
        g.wait_ge(ve.h, ve_ccpack)
        dGm.inc(g.dma_start(out=cc_in[:, :], in_=ccpack), 16)
        g.wait_ge(dGm.h, dGm.n)
        ccs.inc(g.collective_compute(
            "AllGather", ALU.bypass, replica_groups=[list(range(NCORES))],
            ins=[cc_in.ap().opt()], outs=[cc_out.ap().opt()]))
        for gi in range(len(QGROUPS)):
            h0, gn = QGROUPS[gi]
            for kc in range(NKC):
                s_wq.issue(g, lambda p, kc=kc, h0=h0, gn=gn: [
                    g.dma_start(out=wstr(p)[:, 0:gn * 128],
                                in_=wq_d[kc * 128:(kc + 1) * 128,
                                         h0 * 128:(h0 + gn) * 128])])
        g.wait_ge(ccs.h, ccs.n)
        for j in range(NCORES - 1, -1, -1):
            s_gth.issue(g, lambda p, j=j: [
                g.dma_start(out=gth(p), in_=cc_out[j * 128:(j + 1) * 128, :])])

    @blk.sync
    def _(sy):
        for kc in range(NKC):
            s_wt1.issue(sy, lambda p, kc=kc: [
                sy.dma_start(out=wtf(p), in_=wt_d[kc * 128:(kc + 1) * 128, :])])

    ve_cmb_base = ve.n
    ve_cmb = {j: ve_cmb_base + 2 * (NCORES - 1 - j) + 1
              for j in range(NCORES - 1, -1, -1)}
    se_cmb = {j: se.n + (NCORES - 1 - j) for j in range(NCORES - 1)}
    pq_cv = PQ()

    @blk.vector
    def _(v):
        for j in range(NCORES - 1, -1, -1):
            p = s_gth.wait(v)
            if j < NCORES - 1:
                pq_cv.put(v.scalar_tensor_tensor(acc8[:, :], bsave[:, :],
                                                 um_s[:, j + 1:j + 2], acc8[:, :],
                                                 ALU.mult, ALU.add))
            inst = v.tensor_copy(bsave[:, :], gth(p)[:, 1024:1032])
            assert ve.inc(inst) == ve_cmb[j]
            if j < NCORES - 1:
                v.wait_ge(se.h, se_cmb[j])
            for kvh in range(NKV):
                wcol = um_s[:, j:j + 1] if j == NCORES - 1 else w8[:, kvh:kvh + 1]
                inst = v.scalar_tensor_tensor(
                    S_init(kvh), gth(p)[:, kvh * 128:(kvh + 1) * 128],
                    wcol, S_init(kvh), ALU.mult, ALU.add)
            ve.inc(inst)
            if j > 0:
                pq_cv.defer(s_gth.done)
            else:
                s_gth.done_skip()
        assert pq_cv.empty()
    ve_comb = ve.n

    @blk.tensor
    def _(t):
        t.wait_ge(ve.h, ve_ccpack)
        for gi, (h0, gn) in enumerate(QGROUPS):
            r_qj.prod_wait(t)
            for kc in range(NKC):
                p = s_wq.wait(t)
                for hh in range(gn):
                    for half in range(2):
                        inst = t.matmul(pview(hh * 2 + half),
                                        wstr(p)[:, hh * 128:(hh + 1) * 128],
                                        x_bf(kc)[:, half * 512:(half + 1) * 512],
                                        start=(kc == 0), stop=(kc == NKC - 1))
                        if (kc, hh, half) == (0, 0, 0):
                            pq_t.put(inst)
                if kc < NKC - 1:
                    s_wq.done(inst)
            r_qj.prod_done(inst)
            if gi < len(QGROUPS) - 1:
                pq_t.defer(s_wq.done)
            else:
                s_wq.done_skip()

    @blk.scalar
    def _(s):
        for gi, (h0, gn) in enumerate(QGROUPS):
            r_qj.cons_wait(s)
            for hh in range(gn):
                qh = h0 + hh
                for half in range(2):
                    inst = s.activation(q_bf(qh)[:, half * 512:(half + 1) * 512],
                                        pview(hh * 2 + half), AF.Relu,
                                        bias=bq_s[:, qh:qh + 1], scale=1.0)
            r_qj.cons_done(inst)

    @blk.scalar
    def _(s):
        for j in range(NCORES - 2, -1, -1):
            s.wait_ge(ve.h, ve_cmb[j])
            s.activation(w8[:, :], acc8[:, :], AF.Exp, bias=cb[:, 0:1], scale=S16)
            assert se.inc(s.mul(w8[:, :], w8[:, :], um_s[:, j:j + 1])) == se_cmb[j]


    @blk.vector
    def _(v):
        for kc in range(NKC):
            p = s_wt1.wait(v)
            inst = v.reduce_sum(wsum[:, kc:kc + 1], wtf(p), AX.X,
                                apply_absolute_value=True)
            s_wt1.done(inst)
        v.tensor_copy(rscr, rscr)
        ve.inc(v.reduce_sum(wsum[:, 16:17], wsum[:, 0:16], AX.X))
    ve_wsum = ve.n

    @blk.tensor
    def _(t):
        t.wait_ge(ve.h, ve_wsum)
        tns.inc(t.matmul(psA[0:1, 6 * 512:6 * 512 + 1], ones_cf[:, :],
                         wsum[:, 16:17], start=True, stop=True))
    tns_alpha = tns.n

    @blk.vector
    def _(v):
        pad = lambda: v.tensor_copy(rscr, rscr)
        v.wait_ge(tns.h, tns_alpha)
        v.tensor_copy(a11[0:1, 3:4], psA[0:1, 6 * 512:6 * 512 + 1])
        pad()
        v.tensor_scalar(a11[0:1, 0:1], a11[0:1, 3:4],
                        2.0 ** -22, 1e-8, ALU.mult, ALU.max)
        pad()
        v.reciprocal(a11[0:1, 1:2], a11[0:1, 0:1])
        pad()
        v.tensor_mul(a11[0:1, 2:3], a11[0:1, 0:1], a11[0:1, 1:2])
        pad()
        v.tensor_scalar(a11[0:1, 2:3], a11[0:1, 2:3], -1.0, 2.0, ALU.mult, ALU.add)
        pad()
        v.tensor_mul(a11[0:1, 1:2], a11[0:1, 1:2], a11[0:1, 2:3])
        pad()
        ve.inc(v.tensor_scalar(a11[0:1, 2:3], a11[0:1, 1:2], -1.0, None, ALU.mult))
    ve_a11 = ve.n

    @blk.tensor
    def _(t):
        t.wait_ge(ve.h, ve_a11)
        t.wait_ge(ge.h, ge.n)
        t.matmul(psA[:, 7 * 512:7 * 512 + 1], ones_rf[0:1, :], a11[0:1, 2:3],
                 start=True, stop=False)
        tns.inc(t.matmul(psA[:, 7 * 512 + 1:7 * 512 + 2], ones_rf[0:1, :],
                         a11[0:1, 0:1], start=False, stop=True))
    tns_ab = tns.n

    @blk.vector
    def _(v):
        v.wait_ge(tns.h, tns_ab)
        ve.inc(v.tensor_copy(alpha_c[:, 0:2], psA[:, 7 * 512:7 * 512 + 2]))
    ge_alpha = None
    ve_alpha = ve.n

    r_O.prefill(NKV * NCH * 2)
    r_A.prefill(NKV * NCH * 2)
    r_stp.prefill(NKV * NCH)

    @blk.scalar
    def _(s):
        for kvh in range(NKV):
            for c in range(NCH):
                p = r_ebem.prod_wait(s)
                r_ebem.prod_done(
                    s.activation(eb_f(p), B_t(kvh)[:, c * 128:(c + 1) * 128], AF.Exp,
                                 bias=bqt[:, kvh * 8 + c:kvh * 8 + c + 1], scale=S16))

    @blk.vector
    def _(v):
        for kvh in range(NKV):
            for c in range(NCH):
                tsl = r_T.prod_wait(v)
                r_T.prod_done(v.scalar_tensor_tensor(
                    ttot_b(tsl), S_init(kvh), emid[:, kvh * 8 + c:kvh * 8 + c + 1],
                    T_bf(kvh, c), ALU.mult, ALU.add))
                ep = r_ebem.cons_wait(v)
                for qi in range(2):
                    qh = 2 * kvh + qi
                    qsl = r_qt.prod_wait(v)
                    r_qt.prod_done(v.tensor_mul(
                        qt_b(qsl), q_bf(qh)[:, c * 128:(c + 1) * 128], eb_f(ep)))
                    asl = r_A.cons_wait(v)
                    bsl = r_Ab.prod_wait(v)
                    r_Ab.prod_done(v.tensor_mul(A_b(bsl), pview(6 + asl, 128),
                                                maskU[:, :]))
                    osl = rO_ve.wait(v)
                    oslice = o_bf(qh)[:, c * 128:(c + 1) * 128]
                    cp = v.tensor_copy(oslice, pview(4 + osl, 128))
                    if qi == 1:
                        r_ebem.cons_done(cp)
                    ssl = r_sq.prod_wait(v)
                    r_sq.prod_done(v.tensor_mul(osq_b(ssl), oslice, oslice))
                stsl = r_stp.cons_wait(v)
                inst = v.tensor_add(stat_sb[:, c * 128:(c + 1) * 128],
                                    stat_sb[:, c * 128:(c + 1) * 128],
                                    pview(2 + stsl, 128)[0:16, :])
        ve.inc(inst)
    ve_oside = ve.n

    @blk.tensor
    def _(t):
        for kvh in range(NKV):
            for c in range(NCH):
                tsl = r_T.cons_wait(t)
                stsl = r_stp.prod_wait(t)
                for qi in range(2):
                    qh = 2 * kvh + qi
                    qsl = r_qt.cons_wait(t)
                    asl = r_A.prod_wait(t)
                    r_A.prod_done(t.matmul(pview(6 + asl, 128),
                                           kt_t(kvh)[:, c * 128:(c + 1) * 128],
                                           qt_b(qsl), start=True, stop=True))
                    bsl = r_Ab.cons_wait(t)
                    osl = r_O.prod_wait(t)
                    t.matmul(pview(4 + osl, 128),
                             v_tm(c)[:, kvh * 128:(kvh + 1) * 128], A_b(bsl),
                             start=True, stop=False)
                    r_O.prod_done(t.matmul(pview(4 + osl, 128), ttot_b(tsl),
                                           qt_b(qsl), start=False, stop=True))
                    ssl = r_sq.cons_wait(t)
                    inst = t.matmul(pview(2 + stsl, 128)[0:16, :],
                                    sel[:, qh * 16:(qh + 1) * 16], osq_b(ssl),
                                    start=(qi == 0), stop=(qi == 1))
                    if qi == 1:
                        r_stp.prod_done(inst)

    @blk.vector
    def _(v):
        v.wait_ge(ve.h, ve_oside)
        v.tensor_scalar(stat_sb, stat_sb, 1.0 / 128.0, 1e-5, ALU.mult, ALU.add)
        ve.inc(v.reciprocal(r_sb, stat_sb))
    ve_gn1 = ve.n

    @blk.scalar
    def _(s):
        s.wait_ge(ve.h, ve_gn1)
        se.inc(s.activation(r_sb, r_sb, AF.Sqrt, bias=cb[0:16, 0:1], scale=1.0))
    se_gn = se.n

    @blk.vector
    def _(v):
        v.wait_ge(se.h, se_gn)
        v.tensor_mul(rscr, r_sb, r_sb)
        v.tensor_mul(rscr, rscr, stat_sb)
        v.tensor_scalar(rscr, rscr, -0.5, 1.5, ALU.mult, ALU.add)
        v.tensor_mul(r_sb, r_sb, rscr)
        v.tensor_copy(rhl[:, 0:1024], r_sb)
        v.tensor_sub(rscr, r_sb, rhl[:, 0:1024])
        ve.inc(v.tensor_copy(rhl[:, 1024:2048], rscr))
    ve_rhl = ve.n

    @blk.sync
    def _(sy):
        sy.wait_ge(ve.h, ve_rhl)
        for rnd in range(6):
            h0 = rnd * 3
            nh = min(3, 16 - h0)
            s_rst.issue(sy, lambda p, h0=h0, nh=nh: [
                sy.dma_start(out=rstage(p)[0:32 * (nh - 1) + 1:32, :],
                             in_=rhl[h0:h0 + nh, :])])

    @blk.tensor
    def _(t):
        t.wait_ge(ve.h, ve_oside)
        for rnd in range(6):
            h0 = rnd * 3
            nh = min(3, 16 - h0)
            p = s_rst.wait(t)
            for hr in range(nh):
                base = 32 * hr
                rsl = r_rb.prod_wait(t)
                for half in range(2):
                    pv = pview(rsl * 2 + half, 512)
                    inst0 = t.matmul(pv, ones_b3[base:base + 1, :],
                                     rstage(p)[base:base + 1,
                                               half * 512:(half + 1) * 512],
                                     start=True, stop=False)
                    if (hr, half) == (0, 0):
                        pq_t.put(inst0)
                    inst = t.matmul(pv, ones_b3[base:base + 1, :],
                                    rstage(p)[base:base + 1,
                                              1024 + half * 512:1024 + (half + 1) * 512],
                                    start=False, stop=True)
                r_rb.prod_done(inst)
            if rnd < 4:
                pq_t.defer(s_rst.done)
            else:
                s_rst.done_skip()
        assert pq_t.empty()

    @blk.vector
    def _(v):
        for qh in range(NQH):
            rsl = r_rb.cons_wait(v)
            inst = v.scalar_tensor_tensor(o_bf(qh), psA[:, rsl * 1024:(rsl + 1) * 1024],
                                          gnw_s[:, 0:1], o_bf(qh), ALU.mult, ALU.mult)
            if qh < NQH - 1:
                r_rb.cons_done(inst)
            else:
                ve.inc(inst)
                pq_v.defer(r_rb.cons_done)
    ve_gnorm = ve.n

    @blk.gpsimd
    def _(g):
        first = [True]
        for fh in range(4):
            for kcg in range(4):
                if first[0] and (fh, kcg) == (0, 1):
                    g.wait_ge(ve.h, ve_oside)
                    first[0] = False
                s_wo.issue(g, lambda p, fh=fh, kcg=kcg: [
                    g.dma_start(out=wostr(p)[:, i * 512:(i + 1) * 512],
                                in_=wo_d[(kcg * 4 + i) * 128:(kcg * 4 + i + 1) * 128,
                                         fh * 512:(fh + 1) * 512])
                    for i in range(4)])

    @blk.tensor
    def _(t):
        t.wait_ge(ve.h, ve_gnorm)
        for fh in range(4):
            r_wop.prod_wait(t)
            for kcg in range(4):
                p = s_wo.wait(t)
                for i in range(4):
                    kc = kcg * 4 + i
                    for ts in range(NTS):
                        inst = t.matmul(pview(ts), o_bf(kc)[:, ts * 128:(ts + 1) * 128],
                                        wostr(p)[:, i * 512:(i + 1) * 512],
                                        start=(kc == 0), stop=(kc == NKC - 1))
                        if (kcg, i, ts) == (0, 0, 0):
                            pq_t.put(inst)
                if kcg < 3:
                    s_wo.done(inst)
            r_wop.prod_done(inst)
            if fh < 3:
                pq_t.defer(s_wo.done)
            else:
                s_wo.done_skip()
        assert pq_t.empty()
    wo_done_tgt = lambda: r_wop.tgts[3]

    ve_wo_ev = {}

    @blk.vector
    def _(v):
        for fh in range(4):
            r_wop.cons_wait(v)
            for ts in range(NTS):
                inst = v.tensor_copy(xr(ts)[:, fh * 512:(fh + 1) * 512], pview(ts))
                if fh == 3:
                    ve_wo_ev[ts] = ve.inc(inst)
                elif ts == NTS - 1:
                    r_wop.cons_done(inst)
                else:
                    pq_v.put(inst)
        assert pq_v.empty()

    @blk.gpsimd
    def _(g):
        for ts in range(NTS):
            g.wait_ge(ve.h, ve_wo_ev[ts])
            dAcc.inc(g.dma_start(out=xr(ts), in_=xtm[ts * 128:(ts + 1) * 128, :],
                                 accum_op=ALU.add), 16)

    @blk.sync
    def _(sy):
        sy.wait_ge(r_rb.rdy.h, r_rb.tgts[-1])
        for kc in range(NKC):
            s_wt2.issue(sy, lambda p, kc=kc: [
                sy.dma_start(out=wtf(p), in_=wt_d[kc * 128:(kc + 1) * 128, :])])

    Lc = lambda ts: lncol[:, ts * 4:ts * 4 + 4]
    B2c = lambda ts: bn2[:, ts * 2:ts * 2 + 2]

    def ln_batch_a(v, wait_fn):
        for ts in range(NTS):
            wait_fn(v, ts)
            for qq in range(4):
                v.bn_stats(bn6[:, ts * 24 + qq * 6:ts * 24 + (qq + 1) * 6],
                           xr(ts)[:, qq * 512:(qq + 1) * 512])
        for ts in range(NTS):
            v.bn_aggr(B2c(ts), bn6[:, ts * 24:ts * 24 + 24])
        for ts in range(NTS):
            v.tensor_scalar(Lc(ts)[:, 0:1], B2c(ts)[:, 1:2], 1e-5, None, ALU.add)
        for ts in range(NTS):
            ve.inc(v.reciprocal(Lc(ts)[:, 1:2], Lc(ts)[:, 0:1]))

    def ln_batch_b(v, wait_fn):
        for ts in range(NTS):
            wait_fn(v, ts)
            v.tensor_mul(Lc(ts)[:, 2:3], Lc(ts)[:, 1:2], Lc(ts)[:, 1:2])
        for ts in range(NTS):
            v.tensor_mul(Lc(ts)[:, 2:3], Lc(ts)[:, 2:3], Lc(ts)[:, 0:1])
        for ts in range(NTS):
            v.tensor_scalar(Lc(ts)[:, 2:3], Lc(ts)[:, 2:3], -0.5, 1.5,
                            ALU.mult, ALU.add)
        for ts in range(NTS):
            v.tensor_mul(Lc(ts)[:, 1:2], Lc(ts)[:, 1:2], Lc(ts)[:, 2:3])
        for ts in range(NTS):
            ve.inc(v.scalar_tensor_tensor(Lc(ts)[:, 2:3], B2c(ts)[:, 0:1], -1.0,
                                          Lc(ts)[:, 1:2], ALU.mult, ALU.mult))

    @blk.vector
    def _(v):
        ln_batch_a(v, lambda vv, ts: vv.wait_ge(dAcc.h, 16 * (ts + 1)))
    ve_ln1a0 = ve.n - NTS

    @blk.scalar
    def _(s):
        for ts in range(NTS):
            s.wait_ge(ve.h, ve_ln1a0 + ts + 1)
            L = lncol[:, ts * 4:ts * 4 + 4]
            se.inc(s.activation(L[:, 1:2], L[:, 1:2], AF.Sqrt,
                                bias=cb[:, 0:1], scale=1.0))
    se_ln1s0 = se.n - NTS

    @blk.vector
    def _(v):
        ln_batch_b(v, lambda vv, ts: vv.wait_ge(se.h, se_ln1s0 + ts + 1))
    ve_ln1b0 = ve.n - NTS

    se_x2 = {}

    @blk.scalar
    def _(s):
        for ts in range(NTS):
            s.wait_ge(ve.h, ve_ln1b0 + ts + 1)
            if dbg_x2:
                s.wait_ge(dTap.h, 16 * (10 + ts))
            L = lncol[:, ts * 4:ts * 4 + 4]
            se_x2[ts] = se.inc(s.activation(xr(ts), xr(ts), AF.Identity,
                                            bias=L[:, 2:3], scale=L[:, 1:2]))

    if dbg_x2:
        @blk.sync
        def _(sy):
            sy.wait_ge(ve.h, ve_oside)
            dTap.inc(sy.dma_start(out=tapT7[:, :], in_=AR7[:, :]), 16)
            dTap.inc(sy.dma_start(out=tapB[:, :], in_=AR3[:, 0:1024]), 16)
            for nm, off, w in [(mtab, 0, 64), (ftab, 64, 64), (bkt, 128, 64),
                               (bqt, 192, 64), (emid, 256, 64), (sendf, 320, 8)]:
                dTap.inc(sy.dma_start(out=tapTab[:, off:off + w], in_=nm[:, :]), 16)
            sy.wait_ge(ve.h, ve_gnorm)
            dTap.inc(sy.dma_start(out=tapo[:, :], in_=AR1[:, :]), 16)
            for ts in range(NTS):
                sy.wait_ge(dAcc.h, 16 * (ts + 1))
                dTap.inc(sy.dma_start(out=tapxr[ts * 128:(ts + 1) * 128, :],
                                      in_=xr(ts)), 16)
            for ts in range(NTS):
                sy.wait_ge(se.h, se_x2[ts])
                dTap.inc(sy.dma_start(out=tapx2[ts * 128:(ts + 1) * 128, :],
                                      in_=xr(ts)), 16)



    @blk.scalar
    def _(s):
        s.wait_ge(ve.h, ve_alpha)
        s.wait_ge(ve.h, ve_oside)
        if dbg_x2:
            s.wait_ge(dTap.h, 16 * 8)
        for kc in range(NKC):
            p = s_wt2.wait(s)
            for hq in range(4):
                uq = r_uq.prod_wait(s)
                inst = s.activation(uw_f32(uq), wtf(p)[:, hq * 512:(hq + 1) * 512],
                                    AF.Identity, bias=cb[:, 0:1],
                                    scale=alpha_c[:, 0:1])
                r_uq.prod_done(inst)

    @blk.vector
    def _(v):
        v.wait_ge(r_wop.rdy.h, wo_done_tgt())
        for kc in range(NKC):
            for hq in range(4):
                uq = r_uq.cons_wait(v)
                cp = v.tensor_copy(iw_i32(uq), uw_f32(uq))
                if hq == 3 and kc < NKC - 1:
                    s_wt2.done(cp)
                inst = v.tensor_scalar(wtq(kc)[:, hq * 512:(hq + 1) * 512],
                                       iw_i32(uq), -1.0, 1.0, ALU.max, ALU.min)
                if (kc, hq) == (NKC - 1, 3):
                    ve.inc(inst)
                else:
                    r_uq.cons_done(inst)
        s_wt2.done_skip()
    ve_wq = ve.n

    ve_bq0 = {}

    @blk.vector
    def _(v):
        for ts in range(NTS):
            v.wait_ge(se.h, se_x2[ts])
            v.reduce_max(abcols[:, 2 * ts:2 * ts + 1], xr(ts), AX.X,
                         apply_absolute_value=True)
        for ts in range(NTS):
            v.tensor_scalar(abcols[:, 2 * ts:2 * ts + 1],
                            abcols[:, 2 * ts:2 * ts + 1], 1.0 / 127.0, 1e-8,
                            ALU.mult, ALU.max)
        for ts in range(NTS):
            v.tensor_mul(abcols[:, 2 * ts + 1:2 * ts + 2],
                         abcols[:, 2 * ts:2 * ts + 1], alpha_c[:, 1:2])
        for ts in range(NTS):
            v.reciprocal(abcols[:, 2 * ts:2 * ts + 1], abcols[:, 2 * ts:2 * ts + 1])
        for ts in range(NTS):
            ve_bq0[ts] = ve.inc(
                v.tensor_scalar(abcols[:, 2 * ts:2 * ts + 1],
                                abcols[:, 2 * ts:2 * ts + 1], -1.0, None, ALU.mult))

    if dbg_x2:
        @blk.sync
        def _(sy):
            sy.wait_ge(ve.h, ve_bq0[NTS - 1])
            dTap.inc(sy.dma_start(out=tapAB[:, 0:16], in_=abcols[:, :]), 16)
            dTap.inc(sy.dma_start(out=tapAB[:, 16:18], in_=alpha_c[:, :]), 16)
            dTap.inc(sy.dma_start(out=tapAB[0:1, 18:22], in_=a11[:, :]), 16)

    @blk.scalar
    def _(s):
        s.wait_ge(r_wop.rdy.h, wo_done_tgt())
        if dbg_x2:
            s.wait_ge(dTap.h, 16 * 9)
        for ts in range(NTS):
            s.wait_ge(ve.h, ve_bq0[ts])
            for hq in range(2):
                uq = r_ux.prod_wait(s)
                r_ux.prod_done(
                    s.activation(ux_f32(uq), xr(ts)[:, hq * 1024:(hq + 1) * 1024],
                                 AF.Identity, bias=cb[:, 0:1],
                                 scale=abcols[:, 2 * ts:2 * ts + 1]))

    dT_x0 = dT.n

    @blk.vector
    def _(v):
        for ts in range(NTS):
            if ts >= 2:
                v.wait_ge(dT.h, dT_x0 + 256 * (ts - 1))
            xsl = r_xq.prod_wait(v)
            for hq in range(2):
                uq = r_ux.cons_wait(v)
                cp = v.tensor_copy(ix_i32(uq), ux_f32(uq))
                pq_v.put(cp)
                inst = v.tensor_scalar(x2q_tm(xsl)[:, hq * 1024:(hq + 1) * 1024],
                                       ix_i32(uq), -127.0, 127.0, ALU.max, ALU.min)
                if hq == 1:
                    r_xq.prod_done(inst)
                    pq_v.defer(r_ux.cons_done)
                else:
                    r_ux.cons_done(inst)
        pq_v.q.pop()
        assert pq_v.empty()

    @blk.sync
    def _(sy):
        sy.wait_ge(ve.h, ve_wq)
        sy.wait_ge(r_wop.rdy.h, wo_done_tgt())
        for ts in range(NTS):
            xsl = r_xq.cons_wait(sy)
            for kc in range(NKC):
                inst = sy.dma_start_transpose(
                    x2q_fm(kc)[:, ts * 128:(ts + 1) * 128],
                    x2q_tm(xsl)[:, kc * 128:(kc + 1) * 128])
                dT.inc(inst, 16)
    dT_all = dT.n

    @blk.tensor
    def _(t):
        t.wait_ge(dT.h, dT_all)
        t.wait_ge(ve.h, ve_wq)
        if dbg_x2:
            t.wait_ge(dTap.h, dTap.n)
        for og in range(4):
            r_ffp.prod_wait(t)
            for kc in range(NKC):
                for ts in range(NTS):
                    inst = t.matmul(pview(ts),
                                    x2q_fm(kc)[:, ts * 128:(ts + 1) * 128],
                                    wtq(kc)[:, og * 512:(og + 1) * 512],
                                    start=(kc == 0), stop=(kc == NKC - 1))
            r_ffp.prod_done(inst)

    ve_y_ev = {}

    @blk.vector
    def _(v):
        for og in range(4):
            r_ffp.cons_wait(v)
            for ts in range(NTS):
                inst = v.scalar_tensor_tensor(
                    xr(ts)[:, og * 512:(og + 1) * 512], pview(ts),
                    abcols[:, 2 * ts + 1:2 * ts + 2],
                    xr(ts)[:, og * 512:(og + 1) * 512], ALU.mult, ALU.add)
                if og == 3:
                    ve_y_ev[ts] = ve.inc(inst)
                elif ts == NTS - 1:
                    r_ffp.cons_done(inst)

    @blk.vector
    def _(v):
        ln_batch_a(v, lambda vv, ts: vv.wait_ge(ve.h, ve_y_ev[ts]))
    ve_ln2a0 = ve.n - NTS

    @blk.scalar
    def _(s):
        for ts in range(NTS):
            s.wait_ge(ve.h, ve_ln2a0 + ts + 1)
            L = lncol[:, ts * 4:ts * 4 + 4]
            se.inc(s.activation(L[:, 1:2], L[:, 1:2], AF.Sqrt,
                                bias=cb[:, 0:1], scale=1.0))
    se_ln2s0 = se.n - NTS

    @blk.vector
    def _(v):
        ln_batch_b(v, lambda vv, ts: vv.wait_ge(se.h, se_ln2s0 + ts + 1))
    ve_ln2b0 = ve.n - NTS

    @blk.scalar
    def _(s):
        for ts in range(NTS):
            s.wait_ge(ve.h, ve_ln2b0 + ts + 1)
            if ts >= 2:
                s.wait_ge(dO.h, 16 * (ts - 1))
            L = lncol[:, ts * 4:ts * 4 + 4]
            osl = r_ot.prod_wait(s)
            r_ot.prod_done(s.activation(out_stg(osl), xr(ts), AF.Identity,
                                        bias=L[:, 2:3], scale=L[:, 1:2]))

    @blk.sync
    def _(sy):
        for ts in range(NTS):
            osl = r_ot.cons_wait(sy)
            inst = sy.dma_start(out=out_d[ts * 128:(ts + 1) * 128, :],
                                in_=out_stg(osl))
            dO.inc(inst, 16)
        sy.wait_ge(dO.h, dO.n)

    es.close()
    return nc


def kernel(**inputs):
    x = np.ascontiguousarray(np.asarray(inputs["L0_attn_hidden_states"],
                                        dtype=np.float32))[0]
    x_t = np.ascontiguousarray(x.T)
    wq = np.ascontiguousarray(np.asarray(inputs["L0_attn_Wq"], dtype=np.float32))
    wk = np.ascontiguousarray(np.asarray(inputs["L0_attn_Wk"], dtype=np.float32))
    wv = np.ascontiguousarray(np.asarray(inputs["L0_attn_Wv"], dtype=np.float32))
    wg1 = np.ascontiguousarray(np.asarray(inputs["L0_attn_Wg1"], dtype=np.float32))
    wg2 = np.ascontiguousarray(np.asarray(inputs["L0_attn_Wg2"], dtype=np.float32))
    wo = np.ascontiguousarray(np.asarray(inputs["L0_attn_Wo"], dtype=np.float32))
    wt = np.ascontiguousarray(np.asarray(inputs["L0_ffn_weight"], dtype=np.float32).T)
    bq = np.ascontiguousarray(
        np.asarray(inputs["L0_attn_bq"], dtype=np.float32).reshape(16, 128).T)
    bk = np.ascontiguousarray(
        np.asarray(inputs["L0_attn_bk"], dtype=np.float32).reshape(8, 128).T)
    bg2 = np.ascontiguousarray(
        np.asarray(inputs["L0_attn_bg2"], dtype=np.float32).reshape(8, 128).T)
    bv = np.ascontiguousarray(
        np.asarray(inputs["L0_attn_bv"], dtype=np.float32).reshape(1, 1024))
    gnw = np.ascontiguousarray(
        np.asarray(inputs["L0_attn_g_norm_w"], dtype=np.float32).reshape(128, 1))

    nc = _build()
    in_maps = []
    for c in range(NCORES):
        um = np.zeros((128, 8), dtype=np.float32)
        um[:, :c] = 1.0
        in_maps.append({
            "xfm": np.ascontiguousarray(x_t[:, c * T:(c + 1) * T]),
            "xtm": np.ascontiguousarray(x[c * T:(c + 1) * T, :]),
            "wq": wq, "wk": wk, "wv": wv, "wg1": wg1, "wg2": wg2,
            "wo": wo, "wt": wt, "bqr": bq, "bkr": bk, "bg2r": bg2,
            "bvr": bv, "gnw": gnw, "um": um,
        })
    res = run_bass_kernel_spmd(nc, in_maps, core_ids=list(range(NCORES)))
    out = np.concatenate([np.asarray(r["out"]) for r in res.results], axis=0)
    return out.reshape(1, NCORES * T, HID)


# version 17
# speedup vs baseline: 1.0107x; 1.0047x over previous
import contextlib
import numpy as np

import concourse.bass as bass
import concourse.mybir as mybir
from concourse.bass_utils import run_bass_kernel_spmd

F32 = mybir.dt.float32
BF16 = mybir.dt.bfloat16
I32 = mybir.dt.int32
AF = mybir.ActivationFunctionType
ALU = mybir.AluOpType
AX = mybir.AxisListType

NCORES = 8
HID = 2048
T = 1024
NKC = 16
NKV = 8
NQH = 16
D = 128
NCH = 8
NTS = 8
S16 = 1.0 / 16.0
LNSQD = float(np.log(np.sqrt(128.0)))

KGROUPS = [(0, 3), (3, 3), (6, 2)]
QGROUPS = [(0, 3), (3, 3), (6, 3), (9, 3), (12, 3), (15, 1)]
QGRP_OF_KVH = [0, 1, 1, 2, 3, 3, 4, 5]


class _Sem:
    __slots__ = ("h", "n")

    def __init__(self, es, nc, name):
        self.h = es.enter_context(nc.semaphore(name))
        self.n = 0

    def inc(self, inst, k=1):
        self.n += k
        inst.then_inc(self.h, k)
        return self.n


class Ring:

    def __init__(self, es, nc, name, n, cons_per=1, data_only=False):
        self.rdy = _Sem(es, nc, name + "_r")
        self.fre = None if data_only else _Sem(es, nc, name + "_f")
        self.n = n
        self.cons_per = cons_per
        self.ip = 0
        self.tgts = []
        self._ic0 = 0

    def prod_wait(self, eng):
        if self.fre is not None and self.ip >= self.n:
            eng.wait_ge(self.fre.h, self.cons_per * (self.ip - self.n + 1))
        return self.ip % self.n

    def prod_done(self, inst):
        self.rdy.inc(inst)
        if self.ip < len(self.tgts):
            assert self.tgts[self.ip] == self.rdy.n
        else:
            self.tgts.append(self.rdy.n)
        self.ip += 1

    def prefill(self, count):
        self.tgts = self.tgts + [self.rdy.n + i + 1 for i in range(count)]

    def cons_wait(self, eng):
        slot = self._ic0 % self.n
        eng.wait_ge(self.rdy.h, self.tgts[self._ic0])
        self._ic0 += 1
        return slot

    def cons_done(self, inst):
        self.fre.inc(inst)

    def consumer(self):
        ring = self

        class _C:
            def __init__(self):
                self.ic = 0

            def wait(self, eng):
                slot = self.ic % ring.n
                eng.wait_ge(ring.rdy.h, ring.tgts[self.ic])
                self.ic += 1
                return slot

        return _C()


class TileStream:
    def __init__(self, es, nc, name, nbuf=2):
        self.ld = [_Sem(es, nc, f"{name}_l{i}") for i in range(nbuf)]
        self.fre = _Sem(es, nc, name + "_f")
        self.nbuf = nbuf
        self.ti = 0
        self.tc = 0
        self.tgts = []

    def issue(self, eng, emit):
        t = self.ti
        p = t % self.nbuf
        if t >= self.nbuf:
            eng.wait_ge(self.fre.h, t - self.nbuf + 1)
        for i in emit(p):
            self.ld[p].inc(i, 16)
        self.tgts.append((p, self.ld[p].n))
        self.ti += 1

    def wait(self, eng):
        p, tgt = self.tgts[self.tc]
        eng.wait_ge(self.ld[p].h, tgt)
        self.tc += 1
        return p

    def done(self, inst):
        self.fre.inc(inst)

    def done_skip(self):
        pass


class PQ:

    def __init__(self):
        self.q = []

    def defer(self, fn):
        self.q.append(fn)

    def put(self, inst, fn=None):
        if fn is not None:
            fn(inst)
        elif self.q:
            self.q.pop(0)(inst)
        return inst

    def empty(self):
        return not self.q


def _build(dbg_x2=False):
    nc = bass.Bass("TRN2", target_bir_lowering=False, debug=False,
                   detect_race_conditions=False)

    def din(name, shape, dtype=F32):
        return nc.dram_tensor(name, shape, dtype, kind="ExternalInput")

    xfm = din("xfm", [HID, T])
    xtm = din("xtm", [T, HID])
    wq_d = din("wq", [HID, HID])
    wk_d = din("wk", [HID, NKV * D])
    wv_d = din("wv", [HID, NKV * D])
    wg1_d = din("wg1", [HID, 16])
    wg2_d = din("wg2", [16, NKV * D])
    wo_d = din("wo", [HID, HID])
    wt_d = din("wt", [HID, HID])
    bqr = din("bqr", [128, 16])
    bkr = din("bkr", [128, 8])
    bg2r = din("bg2r", [128, 8])
    bvr = din("bvr", [1, NKV * D])
    gnw_d = din("gnw", [128, 1])
    um_d = din("um", [128, 8])
    out_d = nc.dram_tensor("out", [T, HID], F32, kind="ExternalOutput")
    cc_in = nc.dram_tensor("cc_in", [128, 1032], F32)
    cc_out = nc.dram_tensor("cc_out", [NCORES * 128, 1032], F32,
                            addr_space="Shared")
    tapx2 = (nc.dram_tensor("tapx2", [T, HID], F32, kind="ExternalOutput")
             if dbg_x2 else None)
    tapo = (nc.dram_tensor("tapo", [128, 16384], mybir.dt.bfloat16, kind="ExternalOutput")
            if dbg_x2 else None)
    tapT7 = (nc.dram_tensor("tapT7", [128, 8192], mybir.dt.bfloat16, kind="ExternalOutput")
             if dbg_x2 else None)
    tapTab = (nc.dram_tensor("tapTab", [128, 328], F32, kind="ExternalOutput")
              if dbg_x2 else None)
    tapB = (nc.dram_tensor("tapB", [128, 1024], F32, kind="ExternalOutput")
            if dbg_x2 else None)
    tapxr = (nc.dram_tensor("tapxr", [T, HID], F32, kind="ExternalOutput")
             if dbg_x2 else None)
    tapAB = (nc.dram_tensor("tapAB", [128, 22], F32, kind="ExternalOutput")
             if dbg_x2 else None)

    es = contextlib.ExitStack()
    sbt = lambda name, shape, dtype: es.enter_context(nc.sbuf_tensor(name, shape, dtype))

    iota_f = sbt("iota_f", [128, 384], BF16)
    maskU = sbt("maskU", [128, 128], BF16)
    sel = sbt("sel", [128, 256], BF16)
    ones_b = sbt("ones_b", [1, 128], BF16)
    ones_b3 = sbt("ones_b3", [65, 128], BF16)
    ones_cf = sbt("ones_cf", [128, 1], F32)
    cb = sbt("cb", [128, 2], F32)
    bq_s = sbt("bq_s", [128, 16], F32)
    bk_s = sbt("bk_s", [128, 8], F32)
    bg2_s = sbt("bg2_s", [128, 8], F32)
    bv_b = sbt("bv_b", [1, NKV * D], BF16)
    gnw_s = sbt("gnw_s", [128, 1], F32)
    um_s = sbt("um_s", [128, 8], F32)
    mtab = sbt("mtab", [128, 64], F32)
    ftab = sbt("ftab", [128, 64], F32)
    bkt = sbt("bkt", [128, 64], F32)
    bqt = sbt("bqt", [128, 64], F32)
    emid = sbt("emid", [128, 64], F32)
    sendf = sbt("sendf", [128, 8], F32)
    acc8 = sbt("acc8", [128, 8], F32)
    w8 = sbt("w8", [128, 8], F32)
    bsave = sbt("bsave", [128, 8], F32)
    alpha_c = sbt("alpha_c", [128, 2], F32)
    a11 = sbt("a11", [1, 4], F32)
    wsum = sbt("wsum", [128, 17], F32)
    abcols = sbt("abcols", [128, 16], F32)
    bn2 = sbt("bn2", [128, 16], F32)
    lncol = sbt("lncol", [128, 32], F32)

    AR1 = sbt("AR1", [128, 16384], BF16)
    AR2 = sbt("AR2", [128, 16384], BF16)
    AR3 = sbt("AR3", [128, 8192], F32)
    AR4 = sbt("AR4", [128, 8192], F32)
    AR5 = sbt("AR5", [128, 8192], BF16)
    AR6 = sbt("AR6", [128, 8192], BF16)
    AR7 = sbt("AR7", [128, 8192], BF16)
    AR8 = sbt("AR8", [128, 8192], BF16)
    AR9 = sbt("AR9", [128, 4864], BF16)

    x_bf = lambda kc: AR1[:, kc * 1024:(kc + 1) * 1024]
    o_bf = lambda qh: AR1[:, qh * 1024:(qh + 1) * 1024]
    ux_f32 = lambda p: AR1.bitcast(F32)[:, p * 1024:(p + 1) * 1024]
    ix_i32 = lambda p: AR1.bitcast(I32)[:, 2048 + p * 1024:2048 + (p + 1) * 1024]
    x2q_tm = lambda p: AR1[:, 8192 + p * 2048:8192 + (p + 1) * 2048]
    out_stg = lambda p: AR1.bitcast(F32)[:, 2048 + p * 2048:2048 + (p + 1) * 2048]

    q_bf = lambda qh: AR2[:, qh * 1024:(qh + 1) * 1024]
    wv_r = lambda kc: AR2[:, kc * 1024:(kc + 1) * 1024]

    B_t = lambda kvh: AR3[:, kvh * 1024:(kvh + 1) * 1024]
    k_rot = lambda p: AR4[:, p * 1024:(p + 1) * 1024]
    T_loc = lambda kvh: AR4[:, 3072 + kvh * 128:3072 + (kvh + 1) * 128]
    S_init = lambda kvh: AR4[:, 4096 + kvh * 128:4096 + (kvh + 1) * 128]
    gth = lambda p: AR4[:, 5120 + p * 1032:5120 + (p + 1) * 1032]
    ccpack = gth(0)
    xr = lambda ts: (AR3 if ts < 4 else AR4)[:, (ts % 4) * 2048:(ts % 4 + 1) * 2048]

    kt_t = lambda kvh: AR5[:, kvh * 1024:(kvh + 1) * 1024]
    v_tm = lambda ts: AR6[:, ts * 1024:(ts + 1) * 1024]
    T_bf = lambda kvh, c: AR7[:, (kvh * 8 + c) * 128:(kvh * 8 + c + 1) * 128]

    def wtq(kc):
        if kc < 8:
            return AR2[:, kc * 2048:(kc + 1) * 2048]
        if kc < 12:
            return AR5[:, (kc - 8) * 2048:(kc - 7) * 2048]
        return AR6[:, (kc - 12) * 2048:(kc - 11) * 2048]

    x2q_fm = lambda kc: (AR7 if kc < 8 else AR8)[:, (kc % 8) * 1024:(kc % 8 + 1) * 1024]
    wtf = lambda p: AR8.bitcast(F32)[:, p * 2048:(p + 1) * 2048]
    stat_sb = AR8.bitcast(F32)[0:16, 0:1024]
    r_sb = AR8.bitcast(F32)[0:16, 1024:2048]
    rscr = AR8.bitcast(F32)[0:16, 2048:3072]
    rhl = AR8[0:16, 6144:8192]
    rstage = lambda p: AR8[:, 2048 + p * 2048:4096 + p * 2048]
    uw_f32 = lambda p: AR7.bitcast(F32)[:, p * 512:(p + 1) * 512]
    iw_i32 = lambda p: AR7.bitcast(I32)[:, 1024 + p * 512:1024 + (p + 1) * 512]

    wg1_b = AR9[:, 0:256]
    bn6 = AR9.bitcast(F32)[:, 128:320]
    wstr = lambda p: AR9[:, 256 + p * 512:256 + (p + 1) * 512]
    ktr = lambda p: AR9[:, 1280 + p * 128:1280 + (p + 1) * 128]
    A_b = lambda p: AR9[:, 1536 + p * 128:1536 + (p + 1) * 128]
    osq_b = lambda p: AR9[:, 1792 + p * 128:1792 + (p + 1) * 128]
    qt_b = lambda p: AR9[:, 2048 + p * 128:2048 + (p + 1) * 128]
    ttot_b = ktr
    eb_f = lambda p: AR9.bitcast(F32)[:, 1152 + p * 128:1152 + (p + 1) * 128]
    wg2_b = AR9[0:16, 2816:3840]
    t_bf16 = AR9[0:16, 1280:2304]
    wostr = lambda p: (AR9[:, 2816:4864] if p == 0 else AR7[:, 4096:6144])

    ones_rf = iota_f.bitcast(F32)[0:1, 0:128]
    ident = iota_f[:, 256:384]
    psA = es.enter_context(nc.psum_tensor("psA", [128, 4096], F32))
    pview = lambda b, n=512: psA[:, b * 512:b * 512 + n]
    ps_bf = psA.bitcast(BF16)
    tr_bf = lambda sl: ps_bf[:, (6 + sl) * 1024:(6 + sl) * 1024 + 128]

    sems = {}
    for s in ["dGm", "dAcc", "dT", "dO", "dR", "tns", "ve", "se", "ge", "ccs"]:
        sems[s] = _Sem(es, nc, s)
    dTap = _Sem(es, nc, "dTap") if dbg_x2 else None
    dGm, dAcc, dT, dO, dR = (sems[k] for k in ["dGm", "dAcc", "dT", "dO", "dR"])
    tns, ve, se, ge, ccs = (sems[k] for k in ["tns", "ve", "se", "ge", "ccs"])

    s_wv = TileStream(es, nc, "swv", nbuf=4)
    s_wk = TileStream(es, nc, "swk")
    s_wq = TileStream(es, nc, "swq")
    s_wt1 = TileStream(es, nc, "swt1")
    s_wt2 = TileStream(es, nc, "swt2")
    s_wo = TileStream(es, nc, "swo")
    s_gth = TileStream(es, nc, "sgth")
    s_rst = TileStream(es, nc, "srst")
    r_vp = Ring(es, nc, "rvp", 2)
    r_kj = Ring(es, nc, "rkj", 1)
    r_ebem = Ring(es, nc, "rebem", 2)
    r_tr = Ring(es, nc, "rtr", 2, data_only=True)
    r_kt = Ring(es, nc, "rkt", 2, data_only=True)
    r_p = Ring(es, nc, "rp", 2, data_only=True)
    r_qj = Ring(es, nc, "rqj", 1)
    r_qt = Ring(es, nc, "rqt", 2, data_only=True)
    r_A = Ring(es, nc, "rA", 2, data_only=True)
    r_Ab = Ring(es, nc, "rAb", 2, data_only=True)
    r_T = Ring(es, nc, "rT", 2, data_only=True)
    r_O = Ring(es, nc, "rO", 2, data_only=True)
    rO_ve = r_O.consumer()
    r_sq = Ring(es, nc, "rsq", 2, data_only=True)
    r_stp = Ring(es, nc, "rstp", 2, data_only=True)
    r_rb = Ring(es, nc, "rrb", 2)
    r_uq = Ring(es, nc, "ruq", 2)
    r_wop = Ring(es, nc, "rwop", 1)
    r_ffp = Ring(es, nc, "rffp", 1)
    r_ux = Ring(es, nc, "rux", 2)
    r_xq = Ring(es, nc, "rxq", 2, data_only=True)
    r_ot = Ring(es, nc, "rot", 2, data_only=True)

    blk = es.enter_context(nc.Block())

    @blk.gpsimd
    def _(g):
        g.iota(iota_f[:, 0:128], [[1, 128]], channel_multiplier=-1,
               allow_small_or_imprecise_dtypes=True)
        g.iota(iota_f[:, 128:384], [[-1, 16], [1, 16]], channel_multiplier=0,
               allow_small_or_imprecise_dtypes=True)
        g.memset(ones_b[:, :], 1.0)
        g.memset(ones_b3[:, :], 1.0)
        g.memset(ones_cf[:, :], 1.0)
        g.memset(cb[:, 0:1], 0.0)
        g.memset(cb[:, 1:2], -LNSQD)
        g.memset(AR7[:, :], 0.0)
        g.memset(AR4[:, 4096:5120], 0.0)
        g.memset(acc8[:, :], 0.0)
        ge.inc(g.memset(stat_sb, 0.0))
        dGm.inc(g.dma_start(out=bq_s[:, :], in_=bqr[:, :]), 16)
        dGm.inc(g.dma_start(out=bk_s[:, :], in_=bkr[:, :]), 16)
        dGm.inc(g.dma_start(out=bg2_s[:, :], in_=bg2r[:, :]), 16)
        dGm.inc(g.dma_start(out=bv_b[:, :], in_=bvr[:, :]), 16)
        dGm.inc(g.dma_start(out=gnw_s[:, :], in_=gnw_d[:, :]), 16)
        dGm.inc(g.dma_start(out=um_s[:, :], in_=um_d[:, :]), 16)
        for j in range(NKC):
            dGm.inc(g.dma_start(out=wg1_b[:, j * 16:(j + 1) * 16],
                                in_=wg1_d[j * 128:(j + 1) * 128, :]), 16)
        dGm.inc(g.dma_start(out=wg2_b[:, :], in_=wg2_d[:, :]), 16)
        dGm_xb = dGm.n
        for kc in range(NKC):
            if kc >= 4:
                g.wait_ge(dGm.h, dGm_xb + 16 * (kc - 3))
            dGm.inc(g.dma_start(out=x_bf(kc), in_=xfm[kc * 128:(kc + 1) * 128, :]), 16)

    dGm_init = dGm.n

    @blk.vector
    def _(v):
        v.wait_ge(ge.h, ge.n)
        v.tensor_scalar(maskU[:, :], iota_f[:, 0:128], 0.0, None, ALU.is_ge)
        v.tensor_scalar(sel[:, :], iota_f[:, 128:384], 0.0, None, ALU.is_equal)
        ve.inc(v.tensor_scalar(ident[:, :], iota_f[:, 0:128], 0.0, None, ALU.is_equal))
    ve_masks = ve.n

    @blk.tensor
    def _(t):
        t.wait_ge(dGm.h, dGm_init)
        t.wait_ge(ve.h, ve_masks)
        for h in range(2):
            for kc in range(NKC):
                inst = t.matmul(psA[0:16, h * 512:(h + 1) * 512],
                                wg1_b[:, kc * 16:(kc + 1) * 16],
                                x_bf(kc)[:, h * 512:(h + 1) * 512],
                                start=(kc == 0), stop=(kc == NKC - 1))
        tns.inc(inst)
    tns_t = tns.n


    @blk.scalar
    def _(s):
        s.wait_ge(tns.h, tns_t)
        se.inc(s.copy(t_bf16[:, :], psA[0:16, 0:1024]))
    se_t = se.n


    @blk.tensor
    def _(t):
        t.wait_ge(se.h, se_t)
        for kvh in range(NKV):
            if kvh >= 1:
                t.wait_ge(se.h, se_t + 2 * kvh)
            for h in range(2):
                inst = t.matmul(pview(2 + h), wg2_b[:, kvh * 128:(kvh + 1) * 128],
                                t_bf16[:, h * 512:(h + 1) * 512],
                                start=True, stop=True)
            tns.inc(inst)
    tns_g0 = tns.n - NKV


    @blk.scalar
    def _(s):
        for kvh in range(NKV):
            s.wait_ge(tns.h, tns_g0 + 1 + kvh)
            if kvh >= 2:
                s.wait_ge(ve.h, ve_masks + 4 * (kvh - 1))
            ls = k_rot(kvh % 2)
            se.inc(s.activation(ls, psA[:, 1024:2048], AF.Sigmoid,
                                bias=bg2_s[:, kvh:kvh + 1], scale=1.0))
            se.inc(s.activation(ls, ls, AF.Ln, bias=cb[:, 0:1], scale=1.0))

    @blk.vector
    def _(v):
        for kvh in range(NKV):
            v.wait_ge(se.h, se_t + 2 * (kvh + 1))
            ve.inc(v.tensor_tensor_scan(B_t(kvh), k_rot(kvh % 2), k_rot(kvh % 2),
                                        0.0, ALU.add, ALU.bypass))
            m8 = mtab[:, kvh * 8:(kvh + 1) * 8]
            ve.inc(v.tensor_copy(m8, B_t(kvh)[:, 63:1024:128]))
            ve.inc(v.tensor_sub(ftab[:, kvh * 8 + 1:kvh * 8 + 8],
                                B_t(kvh)[:, 191:1024:128],
                                B_t(kvh)[:, 63:896:128]))
            ve.inc(v.tensor_sub(sendf[:, kvh:kvh + 1],
                                B_t(kvh)[:, 1023:1024], B_t(kvh)[:, 959:960]))
    ve_B0 = ve.n - 4 * NKV

    @blk.scalar
    def _(s):
        for kvh in range(NKV):
            s.wait_ge(ve.h, ve_B0 + 4 * (kvh + 1))
            c0, c1 = kvh * 8, (kvh + 1) * 8
            s.activation(ftab[:, c0 + 1:c1], ftab[:, c0 + 1:c1], AF.Exp,
                         bias=cb[:, 0:1], scale=S16)
            s.activation(sendf[:, kvh:kvh + 1], sendf[:, kvh:kvh + 1], AF.Exp,
                         bias=cb[:, 0:1], scale=S16)
            s.activation(bkt[:, c0:c1], mtab[:, c0:c1], AF.Identity,
                         bias=cb[:, 0:1], scale=S16)
            s.activation(bqt[:, c0:c1], mtab[:, c0:c1], AF.Identity,
                         bias=cb[:, 1:2], scale=-S16)
            se.inc(s.activation(emid[:, c0:c1], mtab[:, c0:c1], AF.Exp,
                                bias=cb[:, 0:1], scale=S16))
    se_tab0 = se.n - NKV

    @blk.gpsimd
    def _(g):
        for kc in range(0, NKC, 4):
            if kc >= 4:
                p_prev, tgt_prev = s_wv.tgts[kc // 4 - 1]
                g.wait_ge(s_wv.ld[p_prev].h, tgt_prev)
            s_wv.issue(g, lambda p, kc=kc: [
                g.dma_start(out=wv_r(kc + i),
                            in_=wv_d[(kc + i) * 128:(kc + i + 1) * 128, :])
                for i in range(4)])

    @blk.tensor
    def _(t):
        for _ in range(4):
            s_wv.wait(t)
        for ts in range(NTS):
            slot = r_vp.prod_wait(t)
            b0 = 4 + slot * 2
            for kc in range(NKC):
                for h in range(2):
                    t.matmul(pview(b0 + h), x_bf(kc)[:, ts * 128:(ts + 1) * 128],
                             wv_r(kc)[:, h * 512:(h + 1) * 512],
                             start=(kc == 0), stop=False)
            for h in range(2):
                inst = t.matmul(pview(b0 + h), ones_b[0:1, :],
                                bv_b[0:1, h * 512:(h + 1) * 512],
                                start=False, stop=True)
            r_vp.prod_done(inst)
        for _ in range(4):
            s_wv.done_skip()

    @blk.vector
    def _(v):
        for ts in range(NTS):
            slot = r_vp.cons_wait(v)
            b0 = 4 + slot * 2
            inst = v.tensor_copy(v_tm(ts), psA[:, b0 * 512:(b0 + 2) * 512])
            r_vp.cons_done(inst)

    @blk.gpsimd
    def _(g):
        for gi, (h0, gn) in enumerate(KGROUPS):
            for kc in range(NKC):
                s_wk.issue(g, lambda p, kc=kc, h0=h0, gn=gn: [
                    g.dma_start(out=wstr(p)[:, 0:gn * 128],
                                in_=wk_d[kc * 128:(kc + 1) * 128,
                                         h0 * 128:(h0 + gn) * 128])])

    pq_t = PQ()
    pq_v = PQ()

    @blk.tensor
    def _(t):
        t.wait_ge(r_vp.fre.h, NTS)
        t.wait_ge(se.h, se_t + 2 * NKV)
        for gi, (h0, gn) in enumerate(KGROUPS):
            r_kj.prod_wait(t)
            for kc in range(NKC):
                p = s_wk.wait(t)
                for hh in range(gn):
                    for half in range(2):
                        inst = t.matmul(pview(hh * 2 + half),
                                        wstr(p)[:, hh * 128:(hh + 1) * 128],
                                        x_bf(kc)[:, half * 512:(half + 1) * 512],
                                        start=(kc == 0), stop=(kc == NKC - 1))
                        if (kc, hh, half) == (0, 0, 0):
                            pq_t.put(inst)
                if kc < NKC - 1:
                    s_wk.done(inst)
            r_kj.prod_done(inst)
            if gi < len(KGROUPS) - 1:
                pq_t.defer(s_wk.done)
            else:
                s_wk.done_skip()

    @blk.scalar
    def _(s):
        for gi, (h0, gn) in enumerate(KGROUPS):
            r_kj.cons_wait(s)
            for hh in range(gn):
                kvh = h0 + hh
                ks = kvh % 3
                if kvh >= 3:
                    s.wait_ge(r_ebem.fre.h, 8 * (kvh - 2))
                for half in range(2):
                    inst = s.activation(k_rot(ks)[:, half * 512:(half + 1) * 512],
                                        pview(hh * 2 + half), AF.Relu,
                                        bias=bk_s[:, kvh:kvh + 1], scale=1.0)
            r_kj.cons_done(inst)
            for hh in range(gn):
                kvh = h0 + hh
                s.wait_ge(se.h, se_tab0 + kvh + 1)
                for c in range(NCH):
                    p = r_ebem.prod_wait(s)
                    r_ebem.prod_done(
                        s.activation(eb_f(p), B_t(kvh)[:, c * 128:(c + 1) * 128],
                                     AF.Exp,
                                     bias=bkt[:, kvh * 8 + c:kvh * 8 + c + 1],
                                     scale=-S16))

    @blk.vector
    def _(v):
        for gi, (h0, gn) in enumerate(KGROUPS):
            for hh in range(gn):
                kvh = h0 + hh
                ks = kvh % 3
                for c in range(NCH):
                    p = r_ebem.cons_wait(v)
                    inst = v.tensor_mul(kt_t(kvh)[:, c * 128:(c + 1) * 128],
                                        k_rot(ks)[:, c * 128:(c + 1) * 128], eb_f(p))
                    r_ebem.cons_done(inst)
    ktilde_fre = lambda kvh: 8 * (kvh + 1)

    r_kt.prefill(NKV * NCH)

    @blk.tensor
    def _(t):
        for kvh in range(NKV):
            t.wait_ge(r_ebem.fre.h, ktilde_fre(kvh))
            for c in range(NCH):
                sl = r_tr.prod_wait(t)
                r_tr.prod_done(t.transpose(tr_bf(sl),
                                           kt_t(kvh)[:, c * 128:(c + 1) * 128],
                                           ident[:, :]))
                ksl = r_kt.cons_wait(t)
                psl = r_p.prod_wait(t)
                r_p.prod_done(t.matmul(pview(4 + psl, 128), ktr(ksl),
                                       v_tm(c)[:, kvh * 128:(kvh + 1) * 128],
                                       start=True, stop=True))

    @blk.vector
    def _(v):
        for kvh in range(NKV):
            for c in range(NCH):
                sl = r_tr.cons_wait(v)
                ksl = r_kt.prod_wait(v)
                r_kt.prod_done(v.tensor_copy(ktr(ksl), tr_bf(sl)))
                psl = r_p.cons_wait(v)
                pp = pview(4 + psl, 128)
                if c == 0:
                    v.tensor_copy(T_loc(kvh), pp)
                    v.tensor_scalar(T_loc(kvh), T_loc(kvh),
                                    ftab[:, kvh * 8 + 1:kvh * 8 + 2],
                                    None, ALU.mult)
                    v.tensor_copy(T_bf(kvh, 1), T_loc(kvh))
                elif c < NCH - 1:
                    v.tensor_add(T_loc(kvh), pp, T_loc(kvh))
                    v.tensor_scalar(T_loc(kvh), T_loc(kvh),
                                    ftab[:, kvh * 8 + c + 1:kvh * 8 + c + 2],
                                    None, ALU.mult)
                    v.tensor_copy(T_bf(kvh, c + 1), T_loc(kvh))
                else:
                    v.tensor_add(T_loc(kvh), pp, T_loc(kvh))
                    v.tensor_scalar(ccpack[:, kvh * 128:(kvh + 1) * 128],
                                    T_loc(kvh), sendf[:, kvh:kvh + 1], None, ALU.mult)
            ve.inc(v.tensor_copy(ccpack[:, 1024 + kvh:1025 + kvh],
                                 B_t(kvh)[:, 1023:1024]))
    ve_ccpack = ve.n

    def issue_qgroup(g, gi):
        h0, gn = QGROUPS[gi]
        for kc in range(NKC):
            s_wq.issue(g, lambda p, kc=kc, h0=h0, gn=gn: [
                g.dma_start(out=wstr(p)[:, 0:gn * 128],
                            in_=wq_d[kc * 128:(kc + 1) * 128,
                                     h0 * 128:(h0 + gn) * 128])])

    @blk.gpsimd
    def _(g):
        g.wait_ge(ve.h, ve_masks)
        ge.inc(g.memset(ones_rf, 1.0))
        g.wait_ge(ve.h, ve_ccpack)
        dGm.inc(g.dma_start(out=cc_in[:, :], in_=ccpack), 16)
        g.wait_ge(dGm.h, dGm.n)
        ccs.inc(g.collective_compute(
            "AllGather", ALU.bypass, replica_groups=[list(range(NCORES))],
            ins=[cc_in.ap().opt()], outs=[cc_out.ap().opt()]))
        for gi in range(len(QGROUPS)):
            issue_qgroup(g, gi)
        g.wait_ge(ccs.h, ccs.n)
        for j in range(NCORES - 1, -1, -1):
            s_gth.issue(g, lambda p, j=j: [
                g.dma_start(out=gth(p), in_=cc_out[j * 128:(j + 1) * 128, :])])
        for kc in range(NKC):
            s_wt1.issue(g, lambda p, kc=kc: [
                g.dma_start(out=wtf(p), in_=wt_d[kc * 128:(kc + 1) * 128, :])])

    ve_cmb_base = ve.n
    ve_cmb = {j: ve_cmb_base + 2 * (NCORES - 1 - j) + 1
              for j in range(NCORES - 1, -1, -1)}
    se_cmb = {j: se.n + (NCORES - 1 - j) for j in range(NCORES - 1)}
    pq_cv = PQ()

    @blk.vector
    def _(v):
        for j in range(NCORES - 1, -1, -1):
            p = s_gth.wait(v)
            if j < NCORES - 1:
                pq_cv.put(v.scalar_tensor_tensor(acc8[:, :], bsave[:, :],
                                                 um_s[:, j + 1:j + 2], acc8[:, :],
                                                 ALU.mult, ALU.add))
            inst = v.tensor_copy(bsave[:, :], gth(p)[:, 1024:1032])
            assert ve.inc(inst) == ve_cmb[j]
            if j < NCORES - 1:
                v.wait_ge(se.h, se_cmb[j])
            for kvh in range(NKV):
                wcol = um_s[:, j:j + 1] if j == NCORES - 1 else w8[:, kvh:kvh + 1]
                inst = v.scalar_tensor_tensor(
                    S_init(kvh), gth(p)[:, kvh * 128:(kvh + 1) * 128],
                    wcol, S_init(kvh), ALU.mult, ALU.add)
            ve.inc(inst)
            if j > 0:
                pq_cv.defer(s_gth.done)
            else:
                s_gth.done_skip()
        assert pq_cv.empty()
    ve_comb = ve.n

    @blk.tensor
    def _(t):
        t.wait_ge(ve.h, ve_ccpack)
        for gi, (h0, gn) in enumerate(QGROUPS):
            r_qj.prod_wait(t)
            for kc in range(NKC):
                p = s_wq.wait(t)
                for hh in range(gn):
                    for half in range(2):
                        inst = t.matmul(pview(hh * 2 + half),
                                        wstr(p)[:, hh * 128:(hh + 1) * 128],
                                        x_bf(kc)[:, half * 512:(half + 1) * 512],
                                        start=(kc == 0), stop=(kc == NKC - 1))
                        if (kc, hh, half) == (0, 0, 0):
                            pq_t.put(inst)
                if kc < NKC - 1:
                    s_wq.done(inst)
            r_qj.prod_done(inst)
            if gi < len(QGROUPS) - 1:
                pq_t.defer(s_wq.done)
            else:
                s_wq.done_skip()

    @blk.scalar
    def _(s):
        for gi, (h0, gn) in enumerate(QGROUPS):
            r_qj.cons_wait(s)
            for hh in range(gn):
                qh = h0 + hh
                for half in range(2):
                    inst = s.activation(q_bf(qh)[:, half * 512:(half + 1) * 512],
                                        pview(hh * 2 + half), AF.Relu,
                                        bias=bq_s[:, qh:qh + 1], scale=1.0)
            r_qj.cons_done(inst)

    @blk.scalar
    def _(s):
        for j in range(NCORES - 2, -1, -1):
            s.wait_ge(ve.h, ve_cmb[j])
            s.activation(w8[:, :], acc8[:, :], AF.Exp, bias=cb[:, 0:1], scale=S16)
            assert se.inc(s.mul(w8[:, :], w8[:, :], um_s[:, j:j + 1])) == se_cmb[j]


    @blk.vector
    def _(v):
        for kc in range(NKC):
            p = s_wt1.wait(v)
            inst = v.reduce_sum(wsum[:, kc:kc + 1], wtf(p), AX.X,
                                apply_absolute_value=True)
            s_wt1.done(inst)
        v.tensor_copy(rscr, rscr)
        ve.inc(v.reduce_sum(wsum[:, 16:17], wsum[:, 0:16], AX.X))
    ve_wsum = ve.n

    @blk.tensor
    def _(t):
        t.wait_ge(ve.h, ve_wsum)
        tns.inc(t.matmul(psA[0:1, 6 * 512:6 * 512 + 1], ones_cf[:, :],
                         wsum[:, 16:17], start=True, stop=True))
    tns_alpha = tns.n

    @blk.vector
    def _(v):
        pad = lambda: v.tensor_copy(rscr, rscr)
        v.wait_ge(tns.h, tns_alpha)
        v.tensor_copy(a11[0:1, 3:4], psA[0:1, 6 * 512:6 * 512 + 1])
        pad()
        v.tensor_scalar(a11[0:1, 0:1], a11[0:1, 3:4],
                        2.0 ** -22, 1e-8, ALU.mult, ALU.max)
        pad()
        v.reciprocal(a11[0:1, 1:2], a11[0:1, 0:1])
        pad()
        v.tensor_mul(a11[0:1, 2:3], a11[0:1, 0:1], a11[0:1, 1:2])
        pad()
        v.tensor_scalar(a11[0:1, 2:3], a11[0:1, 2:3], -1.0, 2.0, ALU.mult, ALU.add)
        pad()
        v.tensor_mul(a11[0:1, 1:2], a11[0:1, 1:2], a11[0:1, 2:3])
        pad()
        ve.inc(v.tensor_scalar(a11[0:1, 2:3], a11[0:1, 1:2], -1.0, None, ALU.mult))
    ve_a11 = ve.n

    @blk.tensor
    def _(t):
        t.wait_ge(ve.h, ve_a11)
        t.wait_ge(ge.h, ge.n)
        t.matmul(psA[:, 7 * 512:7 * 512 + 1], ones_rf[0:1, :], a11[0:1, 2:3],
                 start=True, stop=False)
        tns.inc(t.matmul(psA[:, 7 * 512 + 1:7 * 512 + 2], ones_rf[0:1, :],
                         a11[0:1, 0:1], start=False, stop=True))
    tns_ab = tns.n

    @blk.vector
    def _(v):
        v.wait_ge(tns.h, tns_ab)
        ve.inc(v.tensor_copy(alpha_c[:, 0:2], psA[:, 7 * 512:7 * 512 + 2]))
    ge_alpha = None
    ve_alpha = ve.n

    r_O.prefill(NKV * NCH * 2)
    r_A.prefill(NKV * NCH * 2)
    r_stp.prefill(NKV * NCH)

    @blk.scalar
    def _(s):
        for kvh in range(NKV):
            for c in range(NCH):
                p = r_ebem.prod_wait(s)
                r_ebem.prod_done(
                    s.activation(eb_f(p), B_t(kvh)[:, c * 128:(c + 1) * 128], AF.Exp,
                                 bias=bqt[:, kvh * 8 + c:kvh * 8 + c + 1], scale=S16))

    @blk.vector
    def _(v):
        for kvh in range(NKV):
            for c in range(NCH):
                tsl = r_T.prod_wait(v)
                r_T.prod_done(v.scalar_tensor_tensor(
                    ttot_b(tsl), S_init(kvh), emid[:, kvh * 8 + c:kvh * 8 + c + 1],
                    T_bf(kvh, c), ALU.mult, ALU.add))
                ep = r_ebem.cons_wait(v)
                for qi in range(2):
                    qh = 2 * kvh + qi
                    qsl = r_qt.prod_wait(v)
                    r_qt.prod_done(v.tensor_mul(
                        qt_b(qsl), q_bf(qh)[:, c * 128:(c + 1) * 128], eb_f(ep)))
                    asl = r_A.cons_wait(v)
                    bsl = r_Ab.prod_wait(v)
                    r_Ab.prod_done(v.tensor_mul(A_b(bsl), pview(6 + asl, 128),
                                                maskU[:, :]))
                    osl = rO_ve.wait(v)
                    oslice = o_bf(qh)[:, c * 128:(c + 1) * 128]
                    cp = v.tensor_copy(oslice, pview(4 + osl, 128))
                    if qi == 1:
                        r_ebem.cons_done(cp)
                    ssl = r_sq.prod_wait(v)
                    r_sq.prod_done(v.tensor_mul(osq_b(ssl), oslice, oslice))
                stsl = r_stp.cons_wait(v)
                inst = v.tensor_add(stat_sb[:, c * 128:(c + 1) * 128],
                                    stat_sb[:, c * 128:(c + 1) * 128],
                                    pview(2 + stsl, 128)[0:16, :])
        ve.inc(inst)
    ve_oside = ve.n

    @blk.tensor
    def _(t):
        for kvh in range(NKV):
            for c in range(NCH):
                tsl = r_T.cons_wait(t)
                stsl = r_stp.prod_wait(t)
                for qi in range(2):
                    qh = 2 * kvh + qi
                    qsl = r_qt.cons_wait(t)
                    asl = r_A.prod_wait(t)
                    r_A.prod_done(t.matmul(pview(6 + asl, 128),
                                           kt_t(kvh)[:, c * 128:(c + 1) * 128],
                                           qt_b(qsl), start=True, stop=True))
                    bsl = r_Ab.cons_wait(t)
                    osl = r_O.prod_wait(t)
                    t.matmul(pview(4 + osl, 128),
                             v_tm(c)[:, kvh * 128:(kvh + 1) * 128], A_b(bsl),
                             start=True, stop=False)
                    r_O.prod_done(t.matmul(pview(4 + osl, 128), ttot_b(tsl),
                                           qt_b(qsl), start=False, stop=True))
                    ssl = r_sq.cons_wait(t)
                    inst = t.matmul(pview(2 + stsl, 128)[0:16, :],
                                    sel[:, qh * 16:(qh + 1) * 16], osq_b(ssl),
                                    start=(qi == 0), stop=(qi == 1))
                    if qi == 1:
                        r_stp.prod_done(inst)

    @blk.vector
    def _(v):
        v.wait_ge(ve.h, ve_oside)
        v.tensor_scalar(stat_sb, stat_sb, 1.0 / 128.0, 1e-5, ALU.mult, ALU.add)
        ve.inc(v.reciprocal(r_sb, stat_sb))
    ve_gn1 = ve.n

    @blk.scalar
    def _(s):
        s.wait_ge(ve.h, ve_gn1)
        se.inc(s.activation(r_sb, r_sb, AF.Sqrt, bias=cb[0:16, 0:1], scale=1.0))
    se_gn = se.n

    @blk.vector
    def _(v):
        v.wait_ge(se.h, se_gn)
        v.tensor_mul(rscr, r_sb, r_sb)
        v.tensor_mul(rscr, rscr, stat_sb)
        v.tensor_scalar(rscr, rscr, -0.5, 1.5, ALU.mult, ALU.add)
        v.tensor_mul(r_sb, r_sb, rscr)
        v.tensor_copy(rhl[:, 0:1024], r_sb)
        v.tensor_sub(rscr, r_sb, rhl[:, 0:1024])
        ve.inc(v.tensor_copy(rhl[:, 1024:2048], rscr))
    ve_rhl = ve.n

    @blk.sync
    def _(sy):
        sy.wait_ge(ve.h, ve_rhl)
        for rnd in range(6):
            h0 = rnd * 3
            nh = min(3, 16 - h0)
            s_rst.issue(sy, lambda p, h0=h0, nh=nh: [
                sy.dma_start(out=rstage(p)[0:32 * (nh - 1) + 1:32, :],
                             in_=rhl[h0:h0 + nh, :])])

    @blk.tensor
    def _(t):
        t.wait_ge(ve.h, ve_oside)
        for rnd in range(6):
            h0 = rnd * 3
            nh = min(3, 16 - h0)
            p = s_rst.wait(t)
            for hr in range(nh):
                base = 32 * hr
                rsl = r_rb.prod_wait(t)
                for half in range(2):
                    pv = pview(rsl * 2 + half, 512)
                    inst0 = t.matmul(pv, ones_b3[base:base + 1, :],
                                     rstage(p)[base:base + 1,
                                               half * 512:(half + 1) * 512],
                                     start=True, stop=False)
                    if (hr, half) == (0, 0):
                        pq_t.put(inst0)
                    inst = t.matmul(pv, ones_b3[base:base + 1, :],
                                    rstage(p)[base:base + 1,
                                              1024 + half * 512:1024 + (half + 1) * 512],
                                    start=False, stop=True)
                r_rb.prod_done(inst)
            if rnd < 4:
                pq_t.defer(s_rst.done)
            else:
                s_rst.done_skip()
        assert pq_t.empty()

    @blk.vector
    def _(v):
        for qh in range(NQH):
            rsl = r_rb.cons_wait(v)
            inst = v.scalar_tensor_tensor(o_bf(qh), psA[:, rsl * 1024:(rsl + 1) * 1024],
                                          gnw_s[:, 0:1], o_bf(qh), ALU.mult, ALU.mult)
            if qh < NQH - 1:
                r_rb.cons_done(inst)
            else:
                ve.inc(inst)
                pq_v.defer(r_rb.cons_done)
    ve_gnorm = ve.n

    @blk.gpsimd
    def _(g):
        first = [True]
        for fh in range(4):
            for kcg in range(4):
                if first[0] and (fh, kcg) == (0, 1):
                    g.wait_ge(ve.h, ve_oside)
                    first[0] = False
                s_wo.issue(g, lambda p, fh=fh, kcg=kcg: [
                    g.dma_start(out=wostr(p)[:, i * 512:(i + 1) * 512],
                                in_=wo_d[(kcg * 4 + i) * 128:(kcg * 4 + i + 1) * 128,
                                         fh * 512:(fh + 1) * 512])
                    for i in range(4)])

    @blk.tensor
    def _(t):
        t.wait_ge(ve.h, ve_gnorm)
        for fh in range(4):
            r_wop.prod_wait(t)
            for kcg in range(4):
                p = s_wo.wait(t)
                for i in range(4):
                    kc = kcg * 4 + i
                    for ts in range(NTS):
                        inst = t.matmul(pview(ts), o_bf(kc)[:, ts * 128:(ts + 1) * 128],
                                        wostr(p)[:, i * 512:(i + 1) * 512],
                                        start=(kc == 0), stop=(kc == NKC - 1))
                        if (kcg, i, ts) == (0, 0, 0):
                            pq_t.put(inst)
                if kcg < 3:
                    s_wo.done(inst)
            r_wop.prod_done(inst)
            if fh < 3:
                pq_t.defer(s_wo.done)
            else:
                s_wo.done_skip()
        assert pq_t.empty()
    wo_done_tgt = lambda: r_wop.tgts[3]

    ve_wo_ev = {}

    @blk.vector
    def _(v):
        for fh in range(4):
            r_wop.cons_wait(v)
            for ts in range(NTS):
                inst = v.tensor_copy(xr(ts)[:, fh * 512:(fh + 1) * 512], pview(ts))
                if fh == 3:
                    ve_wo_ev[ts] = ve.inc(inst)
                elif ts == NTS - 1:
                    r_wop.cons_done(inst)
                else:
                    pq_v.put(inst)
        assert pq_v.empty()

    @blk.gpsimd
    def _(g):
        for ts in range(NTS):
            g.wait_ge(ve.h, ve_wo_ev[ts])
            dAcc.inc(g.dma_start(out=xr(ts), in_=xtm[ts * 128:(ts + 1) * 128, :],
                                 accum_op=ALU.add), 16)
        g.wait_ge(r_rb.rdy.h, r_rb.tgts[-1])
        for kc in range(NKC):
            s_wt2.issue(g, lambda p, kc=kc: [
                g.dma_start(out=wtf(p), in_=wt_d[kc * 128:(kc + 1) * 128, :])])

    Lc = lambda ts: lncol[:, ts * 4:ts * 4 + 4]
    B2c = lambda ts: bn2[:, ts * 2:ts * 2 + 2]

    def ln_batch_a(v, wait_fn):
        for ts in range(NTS):
            wait_fn(v, ts)
            for qq in range(4):
                v.bn_stats(bn6[:, ts * 24 + qq * 6:ts * 24 + (qq + 1) * 6],
                           xr(ts)[:, qq * 512:(qq + 1) * 512])
        for ts in range(NTS):
            v.bn_aggr(B2c(ts), bn6[:, ts * 24:ts * 24 + 24])
        for ts in range(NTS):
            v.tensor_scalar(Lc(ts)[:, 0:1], B2c(ts)[:, 1:2], 1e-5, None, ALU.add)
        for ts in range(NTS):
            ve.inc(v.reciprocal(Lc(ts)[:, 1:2], Lc(ts)[:, 0:1]))

    def ln_batch_b(v, wait_fn):
        for ts in range(NTS):
            wait_fn(v, ts)
            v.tensor_mul(Lc(ts)[:, 2:3], Lc(ts)[:, 1:2], Lc(ts)[:, 1:2])
        for ts in range(NTS):
            v.tensor_mul(Lc(ts)[:, 2:3], Lc(ts)[:, 2:3], Lc(ts)[:, 0:1])
        for ts in range(NTS):
            v.tensor_scalar(Lc(ts)[:, 2:3], Lc(ts)[:, 2:3], -0.5, 1.5,
                            ALU.mult, ALU.add)
        for ts in range(NTS):
            v.tensor_mul(Lc(ts)[:, 1:2], Lc(ts)[:, 1:2], Lc(ts)[:, 2:3])
        for ts in range(NTS):
            ve.inc(v.scalar_tensor_tensor(Lc(ts)[:, 2:3], B2c(ts)[:, 0:1], -1.0,
                                          Lc(ts)[:, 1:2], ALU.mult, ALU.mult))

    @blk.vector
    def _(v):
        ln_batch_a(v, lambda vv, ts: vv.wait_ge(dAcc.h, 16 * (ts + 1)))
    ve_ln1a0 = ve.n - NTS

    @blk.scalar
    def _(s):
        for ts in range(NTS):
            s.wait_ge(ve.h, ve_ln1a0 + ts + 1)
            L = lncol[:, ts * 4:ts * 4 + 4]
            se.inc(s.activation(L[:, 1:2], L[:, 1:2], AF.Sqrt,
                                bias=cb[:, 0:1], scale=1.0))
    se_ln1s0 = se.n - NTS

    @blk.vector
    def _(v):
        ln_batch_b(v, lambda vv, ts: vv.wait_ge(se.h, se_ln1s0 + ts + 1))
    ve_ln1b0 = ve.n - NTS

    se_x2 = {}

    @blk.scalar
    def _(s):
        for ts in range(NTS):
            s.wait_ge(ve.h, ve_ln1b0 + ts + 1)
            if dbg_x2:
                s.wait_ge(dTap.h, 16 * (10 + ts))
            L = lncol[:, ts * 4:ts * 4 + 4]
            se_x2[ts] = se.inc(s.activation(xr(ts), xr(ts), AF.Identity,
                                            bias=L[:, 2:3], scale=L[:, 1:2]))

    if dbg_x2:
        @blk.sync
        def _(sy):
            sy.wait_ge(ve.h, ve_oside)
            dTap.inc(sy.dma_start(out=tapT7[:, :], in_=AR7[:, :]), 16)
            dTap.inc(sy.dma_start(out=tapB[:, :], in_=AR3[:, 0:1024]), 16)
            for nm, off, w in [(mtab, 0, 64), (ftab, 64, 64), (bkt, 128, 64),
                               (bqt, 192, 64), (emid, 256, 64), (sendf, 320, 8)]:
                dTap.inc(sy.dma_start(out=tapTab[:, off:off + w], in_=nm[:, :]), 16)
            sy.wait_ge(ve.h, ve_gnorm)
            dTap.inc(sy.dma_start(out=tapo[:, :], in_=AR1[:, :]), 16)
            for ts in range(NTS):
                sy.wait_ge(dAcc.h, 16 * (ts + 1))
                dTap.inc(sy.dma_start(out=tapxr[ts * 128:(ts + 1) * 128, :],
                                      in_=xr(ts)), 16)
            for ts in range(NTS):
                sy.wait_ge(se.h, se_x2[ts])
                dTap.inc(sy.dma_start(out=tapx2[ts * 128:(ts + 1) * 128, :],
                                      in_=xr(ts)), 16)



    @blk.scalar
    def _(s):
        s.wait_ge(ve.h, ve_alpha)
        s.wait_ge(ve.h, ve_oside)
        if dbg_x2:
            s.wait_ge(dTap.h, 16 * 8)
        for kc in range(NKC):
            p = s_wt2.wait(s)
            for hq in range(4):
                uq = r_uq.prod_wait(s)
                inst = s.activation(uw_f32(uq), wtf(p)[:, hq * 512:(hq + 1) * 512],
                                    AF.Identity, bias=cb[:, 0:1],
                                    scale=alpha_c[:, 0:1])
                r_uq.prod_done(inst)

    @blk.vector
    def _(v):
        v.wait_ge(r_wop.rdy.h, wo_done_tgt())
        for kc in range(NKC):
            for hq in range(4):
                uq = r_uq.cons_wait(v)
                cp = v.tensor_copy(iw_i32(uq), uw_f32(uq))
                if hq == 3 and kc < NKC - 1:
                    s_wt2.done(cp)
                inst = v.tensor_scalar(wtq(kc)[:, hq * 512:(hq + 1) * 512],
                                       iw_i32(uq), -1.0, 1.0, ALU.max, ALU.min)
                if (kc, hq) == (NKC - 1, 3):
                    ve.inc(inst)
                else:
                    r_uq.cons_done(inst)
        s_wt2.done_skip()
    ve_wq = ve.n

    ve_bq0 = {}

    @blk.vector
    def _(v):
        for ts in range(NTS):
            v.wait_ge(se.h, se_x2[ts])
            v.reduce_max(abcols[:, 2 * ts:2 * ts + 1], xr(ts), AX.X,
                         apply_absolute_value=True)
        for ts in range(NTS):
            v.tensor_scalar(abcols[:, 2 * ts:2 * ts + 1],
                            abcols[:, 2 * ts:2 * ts + 1], 1.0 / 127.0, 1e-8,
                            ALU.mult, ALU.max)
        for ts in range(NTS):
            v.tensor_mul(abcols[:, 2 * ts + 1:2 * ts + 2],
                         abcols[:, 2 * ts:2 * ts + 1], alpha_c[:, 1:2])
        for ts in range(NTS):
            v.reciprocal(abcols[:, 2 * ts:2 * ts + 1], abcols[:, 2 * ts:2 * ts + 1])
        for ts in range(NTS):
            ve_bq0[ts] = ve.inc(
                v.tensor_scalar(abcols[:, 2 * ts:2 * ts + 1],
                                abcols[:, 2 * ts:2 * ts + 1], -1.0, None, ALU.mult))

    if dbg_x2:
        @blk.sync
        def _(sy):
            sy.wait_ge(ve.h, ve_bq0[NTS - 1])
            dTap.inc(sy.dma_start(out=tapAB[:, 0:16], in_=abcols[:, :]), 16)
            dTap.inc(sy.dma_start(out=tapAB[:, 16:18], in_=alpha_c[:, :]), 16)
            dTap.inc(sy.dma_start(out=tapAB[0:1, 18:22], in_=a11[:, :]), 16)

    @blk.scalar
    def _(s):
        s.wait_ge(r_wop.rdy.h, wo_done_tgt())
        if dbg_x2:
            s.wait_ge(dTap.h, 16 * 9)
        for ts in range(NTS):
            s.wait_ge(ve.h, ve_bq0[ts])
            for hq in range(2):
                uq = r_ux.prod_wait(s)
                r_ux.prod_done(
                    s.activation(ux_f32(uq), xr(ts)[:, hq * 1024:(hq + 1) * 1024],
                                 AF.Identity, bias=cb[:, 0:1],
                                 scale=abcols[:, 2 * ts:2 * ts + 1]))

    dT_x0 = dT.n

    @blk.vector
    def _(v):
        for ts in range(NTS):
            if ts >= 2:
                v.wait_ge(dT.h, dT_x0 + 256 * (ts - 1))
            xsl = r_xq.prod_wait(v)
            for hq in range(2):
                uq = r_ux.cons_wait(v)
                cp = v.tensor_copy(ix_i32(uq), ux_f32(uq))
                pq_v.put(cp)
                inst = v.tensor_scalar(x2q_tm(xsl)[:, hq * 1024:(hq + 1) * 1024],
                                       ix_i32(uq), -127.0, 127.0, ALU.max, ALU.min)
                if hq == 1:
                    r_xq.prod_done(inst)
                    pq_v.defer(r_ux.cons_done)
                else:
                    r_ux.cons_done(inst)
        pq_v.q.pop()
        assert pq_v.empty()

    @blk.sync
    def _(sy):
        sy.wait_ge(ve.h, ve_wq)
        sy.wait_ge(r_wop.rdy.h, wo_done_tgt())
        for ts in range(NTS):
            xsl = r_xq.cons_wait(sy)
            for kc in range(NKC):
                inst = sy.dma_start_transpose(
                    x2q_fm(kc)[:, ts * 128:(ts + 1) * 128],
                    x2q_tm(xsl)[:, kc * 128:(kc + 1) * 128])
                dT.inc(inst, 16)
    dT_all = dT.n

    @blk.tensor
    def _(t):
        t.wait_ge(dT.h, dT_all)
        t.wait_ge(ve.h, ve_wq)
        if dbg_x2:
            t.wait_ge(dTap.h, dTap.n)
        for og in range(4):
            r_ffp.prod_wait(t)
            for kc in range(NKC):
                for ts in range(NTS):
                    inst = t.matmul(pview(ts),
                                    x2q_fm(kc)[:, ts * 128:(ts + 1) * 128],
                                    wtq(kc)[:, og * 512:(og + 1) * 512],
                                    start=(kc == 0), stop=(kc == NKC - 1))
            r_ffp.prod_done(inst)

    ve_y_ev = {}

    @blk.vector
    def _(v):
        for og in range(4):
            r_ffp.cons_wait(v)
            for ts in range(NTS):
                inst = v.scalar_tensor_tensor(
                    xr(ts)[:, og * 512:(og + 1) * 512], pview(ts),
                    abcols[:, 2 * ts + 1:2 * ts + 2],
                    xr(ts)[:, og * 512:(og + 1) * 512], ALU.mult, ALU.add)
                if og == 3:
                    ve_y_ev[ts] = ve.inc(inst)
                elif ts == NTS - 1:
                    r_ffp.cons_done(inst)

    @blk.vector
    def _(v):
        ln_batch_a(v, lambda vv, ts: vv.wait_ge(ve.h, ve_y_ev[ts]))
    ve_ln2a0 = ve.n - NTS

    @blk.scalar
    def _(s):
        for ts in range(NTS):
            s.wait_ge(ve.h, ve_ln2a0 + ts + 1)
            L = lncol[:, ts * 4:ts * 4 + 4]
            se.inc(s.activation(L[:, 1:2], L[:, 1:2], AF.Sqrt,
                                bias=cb[:, 0:1], scale=1.0))
    se_ln2s0 = se.n - NTS

    @blk.vector
    def _(v):
        ln_batch_b(v, lambda vv, ts: vv.wait_ge(se.h, se_ln2s0 + ts + 1))
    ve_ln2b0 = ve.n - NTS

    @blk.scalar
    def _(s):
        for ts in range(NTS):
            s.wait_ge(ve.h, ve_ln2b0 + ts + 1)
            if ts >= 2:
                s.wait_ge(dO.h, 16 * (ts - 1))
            L = lncol[:, ts * 4:ts * 4 + 4]
            osl = r_ot.prod_wait(s)
            r_ot.prod_done(s.activation(out_stg(osl), xr(ts), AF.Identity,
                                        bias=L[:, 2:3], scale=L[:, 1:2]))

    @blk.sync
    def _(sy):
        for ts in range(NTS):
            osl = r_ot.cons_wait(sy)
            inst = sy.dma_start(out=out_d[ts * 128:(ts + 1) * 128, :],
                                in_=out_stg(osl))
            dO.inc(inst, 16)
        sy.wait_ge(dO.h, dO.n)

    es.close()
    return nc


def kernel(**inputs):
    x = np.ascontiguousarray(np.asarray(inputs["L0_attn_hidden_states"],
                                        dtype=np.float32))[0]
    x_t = np.ascontiguousarray(x.T)
    wq = np.ascontiguousarray(np.asarray(inputs["L0_attn_Wq"], dtype=np.float32))
    wk = np.ascontiguousarray(np.asarray(inputs["L0_attn_Wk"], dtype=np.float32))
    wv = np.ascontiguousarray(np.asarray(inputs["L0_attn_Wv"], dtype=np.float32))
    wg1 = np.ascontiguousarray(np.asarray(inputs["L0_attn_Wg1"], dtype=np.float32))
    wg2 = np.ascontiguousarray(np.asarray(inputs["L0_attn_Wg2"], dtype=np.float32))
    wo = np.ascontiguousarray(np.asarray(inputs["L0_attn_Wo"], dtype=np.float32))
    wt = np.ascontiguousarray(np.asarray(inputs["L0_ffn_weight"], dtype=np.float32).T)
    bq = np.ascontiguousarray(
        np.asarray(inputs["L0_attn_bq"], dtype=np.float32).reshape(16, 128).T)
    bk = np.ascontiguousarray(
        np.asarray(inputs["L0_attn_bk"], dtype=np.float32).reshape(8, 128).T)
    bg2 = np.ascontiguousarray(
        np.asarray(inputs["L0_attn_bg2"], dtype=np.float32).reshape(8, 128).T)
    bv = np.ascontiguousarray(
        np.asarray(inputs["L0_attn_bv"], dtype=np.float32).reshape(1, 1024))
    gnw = np.ascontiguousarray(
        np.asarray(inputs["L0_attn_g_norm_w"], dtype=np.float32).reshape(128, 1))

    nc = _build()
    in_maps = []
    for c in range(NCORES):
        um = np.zeros((128, 8), dtype=np.float32)
        um[:, :c] = 1.0
        in_maps.append({
            "xfm": np.ascontiguousarray(x_t[:, c * T:(c + 1) * T]),
            "xtm": np.ascontiguousarray(x[c * T:(c + 1) * T, :]),
            "wq": wq, "wk": wk, "wv": wv, "wg1": wg1, "wg2": wg2,
            "wo": wo, "wt": wt, "bqr": bq, "bkr": bk, "bg2r": bg2,
            "bvr": bv, "gnw": gnw, "um": um,
        })
    res = run_bass_kernel_spmd(nc, in_maps, core_ids=list(range(NCORES)))
    out = np.concatenate([np.asarray(r["out"]) for r in res.results], axis=0)
    return out.reshape(1, NCORES * T, HID)
